# version 01
# baseline (speedup 1.0000x reference)
import numpy as np
import ml_dtypes
from contextlib import ExitStack

import concourse.bass as bass
import concourse.mybir as mybir
from concourse.bass_utils import run_bass_kernel_spmd
from concourse.alu_op_type import AluOpType as OP

F32 = mybir.dt.float32
BF16 = mybir.dt.bfloat16
AF = mybir.ActivationFunctionType
AX = mybir.AxisListType
BF = ml_dtypes.bfloat16

B, T, C = 2, 8192, 1024
NH, DH = 16, 64
TC = 2048
HALO = 2
TCH = TC + HALO
NCT = C // 128

def _chunks(total, step=512):
    out, o = [], 0
    while o < total:
        n = min(step, total - o)
        out.append((o, n))
        o += n
    return out

CH_TCH = _chunks(TCH)
CH_TC1 = _chunks(TC + 1)


class Ctr:
    def __init__(self, sem, step=1):
        self.sem = sem
        self.n = 0
        self.step = step

    def inc(self, instr, k=1):
        instr.then_inc(self.sem, k * self.step)
        self.n += k * self.step
        return self.n

    def wait(self, eng, val=None):
        eng.wait_ge(self.sem, self.n if val is None else val)


class Ph:
    def __init__(self, nc, stack):
        self.nc = nc
        self.stack = stack
        self.block = stack.enter_context(nc.Block())
        self._cms = {}

    def ctr(self, name, step=1):
        sem = self.stack.enter_context(self.nc.semaphore(name))
        return Ctr(sem, step)

    def sbuf(self, name, shape, dt):
        cm = self.nc.sbuf_tensor(name, shape, dt)
        t = cm.__enter__()
        self._cms[name] = cm
        return t

    def psum(self, name, shape, dt):
        cm = self.nc.psum_tensor(name, shape, dt)
        t = cm.__enter__()
        self._cms[name] = cm
        return t

    def free(self, *names):
        for name in names:
            self._cms.pop(name).__exit__(None, None, None)

    def on(self, engname, fn):
        getattr(self.block, engname)(fn)



def emit_phase0(ph, io):
    ld = ph.ctr("p0ld", 16)
    st = ph.ctr("p0st", 16)
    sv = ph.ctr("p0sv")
    sa = ph.ctr("p0sa")
    stt = ph.ctr("p0st_t")
    sg = ph.ctr("p0sg")

    bank_v = {}
    bank_a = {}

    def D(v, i):
        sv.inc(i)
        v.wait_ge(sv.sem, sv.n)

    consts = ph.sbuf("consts", [128, 8], F32)
    mus = ph.sbuf("mus", [128, 16], F32)
    idb = ph.sbuf("s_idb", [128, 128], BF16)
    hmask = ph.sbuf("s_hmask", [128, 65], BF16)
    onesf = ph.sbuf("onesf", [128, 16], F32)

    def load_const(e):
        ld.inc(e.dma_start(out=idb[:], in_=io["idb"]))
        ld.inc(e.dma_start(out=hmask[:], in_=io["hmask"]))
        for ct in range(NCT):
            ld.inc(e.dma_start(out=mus[:, ct:ct + 1], in_=io["mu_r"][ct * 128:(ct + 1) * 128, :]))
            ld.inc(e.dma_start(out=mus[:, 8 + ct:9 + ct], in_=io["mu_k"][ct * 128:(ct + 1) * 128, :]))
    ph.on("sync", load_const)
    ld_const = ld.n

    def init_const(v):
        v.memset(consts[:, 0:1], 1e-5)
        v.memset(consts[:, 1:2], 1.0 + 1e-6)
        sv.inc(v.memset(onesf[:], 1.0))
    ph.on("vector", init_const)

    c_ln = consts[:, 0:1]
    c_1p = consts[:, 1:2]

    def load_weight_bf(name, dram, wbuf, kdim=1024):
        nt = kdim // 128
        wf = ph.sbuf(name + "_f", [128, 1024], F32)
        snap = (sv.n, sa.n, stt.n, st.n)
        for ctk in range(nt):
            prev_cast = sg.n
            def loadf(e, ctk=ctk, prev_cast=prev_cast, snap=snap):
                if ctk == 0:
                    e.wait_ge(sv.sem, snap[0])
                    e.wait_ge(sa.sem, snap[1])
                    e.wait_ge(stt.sem, snap[2])
                    e.wait_ge(st.sem, snap[3])
                e.wait_ge(sg.sem, prev_cast)
                ld.inc(e.dma_start(out=wf[:], in_=dram[ctk * 128:(ctk + 1) * 128, :]))
            ph.on("sync", loadf)
            ldv = ld.n

            def cast(g, ctk=ctk, ldv=ldv):
                g.wait_ge(ld.sem, ldv)
                sg.inc(g.tensor_copy(wbuf[:, ctk * 1024:(ctk + 1) * 1024], wf[:]))
            ph.on("gpsimd", cast)
        ph.free(name + "_f")
        return sg.n

    xnT = ph.sbuf("xnT", [128, NCT * TCH], BF16)
    vT = ph.sbuf("vT", [128, NCT * TCH], BF16)
    xt = [ph.sbuf(f"xt{i}", [128, 1024], F32) for i in range(2)]
    sqf = ph.sbuf("sqf", [128, 1024], F32)
    sq2 = ph.sbuf("sq2", [128, 1024], F32)
    xnf = ph.sbuf("xnf", [128, 1024], F32)
    xnb = ph.sbuf("xnb", [128, 1024], BF16)
    vb = ph.sbuf("vb", [128, 1024], BF16)
    rw = ph.sbuf("rw", [128, 24], F32)
    ptb = [ph.psum(f"ptb{i}", [128, 128], BF16) for i in range(2)]

    ntile = (TCH + 127) // 128
    tile_clip_done = {}
    tile_tp_done = {}
    tile_v_done = {}

    for i in range(ntile):
        r0 = i * 128
        nr = min(128, TCH - r0)
        x = xt[i % 2]

        def loadx(e, i=i, r0=r0, nr=nr, x=x):
            if i >= 2:
                e.wait_ge(sv.sem, tile_clip_done[i - 2])
            ld.inc(e.dma_start(out=x[:nr, :], in_=io["xs"][r0:r0 + nr, :]))
        ph.on("sync", loadx)
        ldv = ld.n

        def v1(v, x=x, nr=nr, ldv=ldv):
            v.wait_ge(ld.sem, ldv)
            D(v, v.tensor_reduce(rw[:nr, 0:1], x[:nr, :], AX.X, OP.add))
            sv.inc(v.tensor_scalar(rw[:nr, 0:1], rw[:nr, 0:1], 1.0 / C, None, OP.mult))
        ph.on("vector", v1)

        def a1(a, x=x, nr=nr, ldv=ldv, i=i):
            a.wait_ge(ld.sem, ldv)
            if i >= 1:
                a.wait_ge(sv.sem, tile_v_done[i - 1])
            sa.inc(a.activation(sqf[:nr, :], x[:nr, :], AF.Square))
        ph.on("scalar", a1)
        sa_sq = sa.n

        def v2(v, nr=nr, sa_sq=sa_sq):
            v.wait_ge(sa.sem, sa_sq)
            D(v, v.tensor_reduce(rw[:nr, 1:2], sqf[:nr, :], AX.X, OP.add))
            D(v, v.tensor_scalar(rw[:nr, 1:2], rw[:nr, 1:2], 1.0 / C, None, OP.mult))
            D(v, v.tensor_tensor(rw[:nr, 2:3], rw[:nr, 0:1], rw[:nr, 0:1], OP.mult))
            sv.inc(v.tensor_tensor(rw[:nr, 1:2], rw[:nr, 1:2], rw[:nr, 2:3], OP.subtract))
        ph.on("vector", v2)
        sv_var = sv.n

        def a2(a, nr=nr, sv_var=sv_var):
            a.wait_ge(sv.sem, sv_var)
            sa.inc(a.activation(rw[:nr, 3:4], rw[:nr, 1:2], AF.Sqrt, bias=c_ln[:nr]))
        ph.on("scalar", a2)
        sa_sd = sa.n

        def v3(v, x=x, nr=nr, i=i, sa_sd=sa_sd):
            v.wait_ge(sa.sem, sa_sd)
            if i >= 1:
                v.wait_ge(stt.sem, tile_tp_done[i - 1])
            D(v, v.reciprocal(rw[:nr, 4:5], rw[:nr, 3:4]))
            D(v, v.tensor_scalar(xnf[:nr, :], x[:nr, :], rw[:nr, 0:1], rw[:nr, 4:5], OP.subtract, OP.mult))
            v.tensor_copy(xnb[:nr, :], xnf[:nr, :])
            sv.inc(v.tensor_scalar(sqf[:nr, :], xnf[:nr, :], -8.0, 8.0, OP.max, OP.min))
        ph.on("vector", v3)
        sv_xn = sv.n
        tile_clip_done[i] = sv.n

        def a3(a, nr=nr, sv_xn=sv_xn):
            a.wait_ge(sv.sem, sv_xn)
            sa.inc(a.activation(sq2[:nr, :], sqf[:nr, :], AF.Square))
        ph.on("scalar", a3)
        sa_sq2 = sa.n

        def v4(v, nr=nr, sa_sq2=sa_sq2):
            v.wait_ge(sa.sem, sa_sq2)
            sv.inc(v.tensor_reduce(rw[:nr, 5:6], sq2[:nr, 1:1024], AX.X, OP.add))
        ph.on("vector", v4)
        sv_s1 = sv.n

        def a4(a, nr=nr, sv_s1=sv_s1):
            a.wait_ge(sv.sem, sv_s1)
            sa.inc(a.activation(rw[:nr, 6:7], rw[:nr, 5:6], AF.Sqrt, bias=c_1p[:nr]))
        ph.on("scalar", a4)
        sa_x0 = sa.n

        def v5(v, nr=nr, sa_x0=sa_x0):
            v.wait_ge(sa.sem, sa_x0)
            D(v, v.tensor_scalar(rw[:nr, 7:8], rw[:nr, 6:7], 1.0 + 1e-6, None, OP.max))
            sv.inc(v.tensor_scalar(rw[:nr, 8:9], rw[:nr, 7:8], -1.0, None, OP.mult))
        ph.on("vector", v5)
        sv_z = sv.n

        def a5(a, nr=nr, sv_z=sv_z):
            a.wait_ge(sv.sem, sv_z)
            sa.inc(a.activation(rw[:nr, 9:10], rw[:nr, 7:8], AF.Square))
        ph.on("scalar", a5)
        sa_zq = sa.n

        def v6(v, nr=nr, sa_zq=sa_zq):
            v.wait_ge(sa.sem, sa_zq)
            sv.inc(v.tensor_scalar(rw[:nr, 10:11], rw[:nr, 9:10], -1.0, None, OP.add))
        ph.on("vector", v6)
        sv_zq1 = sv.n

        def a6(a, nr=nr, sv_zq1=sv_zq1):
            a.wait_ge(sv.sem, sv_zq1)
            sa.inc(a.activation(rw[:nr, 11:12], rw[:nr, 10:11], AF.Sqrt))
        ph.on("scalar", a6)
        sa_r1 = sa.n

        def v7(v, nr=nr, sa_r1=sa_r1):
            v.wait_ge(sa.sem, sa_r1)
            sv.inc(v.tensor_tensor(rw[:nr, 12:13], rw[:nr, 7:8], rw[:nr, 11:12], OP.add))
        ph.on("vector", v7)
        sv_zr = sv.n

        def a7(a, nr=nr, sv_zr=sv_zr):
            a.wait_ge(sv.sem, sv_zr)
            sa.inc(a.activation(rw[:nr, 13:14], rw[:nr, 12:13], AF.Ln))
        ph.on("scalar", a7)
        sa_ln = sa.n

        def v8(v, nr=nr, sa_ln=sa_ln):
            v.wait_ge(sa.sem, sa_ln)
            v.tensor_scalar(rw[:nr, 14:15], rw[:nr, 13:14], 0.001, None, OP.max)
            sv.inc(v.tensor_tensor(rw[:nr, 15:16], rw[:nr, 6:7], rw[:nr, 8:9], OP.add))
        ph.on("vector", v8)
        sv_d0 = sv.n

        def a8(a, nr=nr, sv_d0=sv_d0):
            a.wait_ge(sv.sem, sv_d0)
            sa.inc(a.activation(rw[:nr, 16:17], rw[:nr, 15:16], AF.Square))
        ph.on("scalar", a8)
        sa_d0q = sa.n

        def v9(v, nr=nr, sa_d0q=sa_d0q):
            v.wait_ge(sa.sem, sa_d0q)
            D(v, v.tensor_tensor(rw[:nr, 17:18], rw[:nr, 5:6], rw[:nr, 16:17], OP.subtract))
            sv.inc(v.tensor_scalar(rw[:nr, 17:18], rw[:nr, 17:18], 1e-6, None, OP.max))
        ph.on("vector", v9)
        sv_t8 = sv.n

        def a9(a, nr=nr, sv_t8=sv_t8):
            a.wait_ge(sv.sem, sv_t8)
            sa.inc(a.activation(rw[:nr, 18:19], rw[:nr, 17:18], AF.Sqrt))
        ph.on("scalar", a9)
        sa_dn = sa.n

        def v10(v, nr=nr, sa_dn=sa_dn):
            v.wait_ge(sa.sem, sa_dn)
            D(v, v.tensor_scalar(rw[:nr, 19:20], rw[:nr, 18:19], 1e-6, None, OP.add))
            D(v, v.reciprocal(rw[:nr, 20:21], rw[:nr, 19:20]))
            D(v, v.tensor_tensor(rw[:nr, 21:22], rw[:nr, 14:15], rw[:nr, 20:21], OP.mult))
            D(v, v.tensor_tensor(rw[:nr, 22:23], rw[:nr, 15:16], rw[:nr, 21:22], OP.mult))
            D(v, v.tensor_scalar(vb[:nr, :], sqf[:nr, :], rw[:nr, 21:22], None, OP.mult))
            sv.inc(v.tensor_copy(vb[:nr, 0:1], rw[:nr, 22:23]))
        ph.on("vector", v10)
        sv_vb = sv.n
        tile_v_done[i] = sv.n

        for (src, dstT, svsrc) in ((xnb, xnT, sv_xn), (vb, vT, sv_vb)):
            for ct in range(NCT):
                pt = ptb[ct % 2]

                def tmm(t, src=src, ct=ct, pt=pt, svsrc=svsrc, nr=nr):
                    t.wait_ge(ld.sem, ld_const)
                    t.wait_ge(sv.sem, max(svsrc, bank_v.get(id(pt), 0)))
                    stt.inc(t.transpose(pt[:, :nr], src[:nr, ct * 128:(ct + 1) * 128], idb[:nr, :nr]))
                ph.on("tensor", tmm)
                t_tp = stt.n

                def tev(v, dstT=dstT, ct=ct, pt=pt, r0=r0, nr=nr, t_tp=t_tp):
                    v.wait_ge(stt.sem, t_tp)
                    sv.inc(v.tensor_copy(dstT[:, ct * TCH + r0: ct * TCH + r0 + nr], pt[:, :nr]))
                ph.on("vector", tev)
                bank_v[id(pt)] = sv.n
        tile_tp_done[i] = stt.n

    sv_vT_done = sv.n
    if "dbg_vT" in io:
        def dbgdump(g):
            g.wait_ge(sv.sem, sv_vT_done)
            for ct in range(NCT):
                st.inc(g.dma_start(out=io["dbg_vT"][ct * 128:(ct + 1) * 128, :], in_=vT[:, ct * TCH:(ct + 1) * TCH]))
                st.inc(g.dma_start(out=io["dbg_xnT"][ct * 128:(ct + 1) * 128, :], in_=xnT[:, ct * TCH:(ct + 1) * TCH]))
        ph.on("gpsimd", dbgdump)
    ph.free("rw", "vb", "xnb", "xnf", "sq2", "sqf", "xt1", "xt0")

    pmm = [ph.psum(f"pmm{i}", [128, 512], F32) for i in range(2)]
    pn = ph.psum("pn", [65, 512], F32)
    qs = [ph.sbuf(f"qs{i}", [128, TCH], BF16) for i in range(2)]
    sqb = ph.sbuf("sqb", [128, 512], BF16)
    x0s = ph.sbuf("x0s", [65, 512], F32)
    stg_free = {0: 0, 1: 0}
    pn_a = [0]

    def emit_proj(wbuf, sg_w, qsign, out_cb):
        for dt in range(NCT):
            stg = qs[dt % 2]
            for (c0, n) in CH_TCH:
                bank = pmm[(c0 // 512) % 2]

                def mmq(t, dt=dt, c0=c0, n=n, bank=bank, sg_w=sg_w):
                    t.wait_ge(sg.sem, sg_w)
                    t.wait_ge(sv.sem, max(sv_vT_done, bank_v.get(id(bank), 0)))
                    t.wait_ge(sa.sem, bank_a.get(id(bank), 0))
                    i = None
                    for ct in range(NCT):
                        i = t.matmul(bank[:, :n], wbuf[:, ct * 1024 + dt * 128:ct * 1024 + (dt + 1) * 128],
                                     vT[:, ct * TCH + c0:ct * TCH + c0 + n],
                                     start=(ct == 0), stop=(ct == NCT - 1))
                    stt.inc(i)
                ph.on("tensor", mmq)
                t_mm = stt.n

                def ev1(v, c0=c0, n=n, bank=bank, stg=stg, dt=dt, t_mm=t_mm):
                    v.wait_ge(stt.sem, t_mm)
                    if c0 == 0:
                        v.wait_ge(st.sem, stg_free[dt % 2])
                    sv.inc(v.tensor_scalar(stg[:, c0:c0 + n], bank[:, :n], -5.0, 5.0, OP.max, OP.min))
                ph.on("vector", ev1)
                sv_clip = sv.n
                bank_v[id(bank)] = sv_clip

                def ea1(a, n=n, stg=stg, c0=c0, sv_clip=sv_clip):
                    a.wait_ge(sv.sem, sv_clip)
                    sa.inc(a.activation(sqb[:, :n], stg[:, c0:c0 + n], AF.Square))
                ph.on("scalar", ea1)
                sa_sqb = sa.n

                def nmm(t, n=n, sa_sqb=sa_sqb):
                    t.wait_ge(ld.sem, ld_const)
                    t.wait_ge(sa.sem, max(sa_sqb, pn_a[0]))
                    stt.inc(t.matmul(pn[:, :n], hmask[:], sqb[:, :n], start=True, stop=True))
                ph.on("tensor", nmm)
                t_nmm = stt.n

                def ea2(a, n=n, t_nmm=t_nmm):
                    a.wait_ge(stt.sem, t_nmm)
                    sa.inc(a.activation(x0s[:, :n], pn[:, :n], AF.Sqrt, bias=c_1p[:65]))
                ph.on("scalar", ea2)
                sa_x0s = sa.n
                pn_a[0] = sa_x0s

                def ev2(v, n=n, stg=stg, c0=c0, sa_x0s=sa_x0s):
                    v.wait_ge(sa.sem, sa_x0s)
                    v.tensor_scalar(stg[0:1, c0:c0 + n], x0s[0:1, :n], qsign, None, OP.mult)
                    sv.inc(v.tensor_scalar(stg[64:65, c0:c0 + n], x0s[64:65, :n], qsign, None, OP.mult))
                ph.on("vector", ev2)
            out_cb(dt, stg, sv.n)

    def q_out(dt, stg, svdep):
        def sq_store(g, dt=dt, stg=stg, svdep=svdep):
            g.wait_ge(sv.sem, svdep)
            st.inc(g.dma_start(out=io["qT"][dt * 128:(dt + 1) * 128, :], in_=stg[:]))
        ph.on("gpsimd", sq_store)
        stg_free[dt % 2] = st.n

    def k_out(dt, stg, svdep):
        def sk_store(g, dt=dt, stg=stg, svdep=svdep):
            g.wait_ge(sv.sem, svdep)
            i = None
            for gb in range(16):
                i = g.dma_start(out=io["kb"][gb, dt * 128:(dt + 1) * 128, :],
                                in_=stg[:, HALO + gb * 128:HALO + (gb + 1) * 128])
                st.inc(i)
        ph.on("gpsimd", sk_store)
        stg_free[dt % 2] = st.n

    wq = ph.sbuf("wq", [128, NCT * 1024], BF16)
    sg_wq = load_weight_bf("wq_ld", io["WqT"], wq)
    emit_proj(wq, sg_wq, 1.0, q_out)
    wk = ph.sbuf("wk", [128, NCT * 1024], BF16)
    sg_wk = load_weight_bf("wk_ld", io["WkT"], wk)
    emit_proj(wk, sg_wk, -1.0, k_out)
    ph.free("wk", "wq", "x0s", "sqb", "qs1", "qs0")

    wv = ph.sbuf("wv", [128, NCT * 1024], BF16)
    sg_wv = load_weight_bf("wv_ld", io["WvT"], wv)
    vvs = [ph.sbuf(f"vvs{i}", [128, 1040], BF16) for i in range(2)]
    vv_free = {0: 0, 1: 0}
    st_after_k = st.n

    ld_after_wv = ld.n
    def vones(v):
        v.wait_ge(st.sem, st_after_k)
        v.wait_ge(ld.sem, ld_after_wv)
        v.wait_ge(sg.sem, sg_wv)
        i = None
        for sbi in range(2):
            for h in range(16):
                i = v.tensor_copy(vvs[sbi][:, h * 65 + 64:h * 65 + 65], onesf[:, h:h + 1])
        sv.inc(i)
    ph.on("vector", vones)

    for tt in range(16):
        stg = vvs[tt % 2]


        for dch in range(2):
            bank = pmm[dch]

            def mmv(t, tt=tt, dch=dch, bank=bank):
                t.wait_ge(sg.sem, sg_wv)
                t.wait_ge(sv.sem, max(sv_vT_done, bank_v.get(id(bank), 0)))
                i = None
                for ct in range(NCT):
                    i = t.matmul(bank[:], vT[:, ct * TCH + HALO + tt * 128: ct * TCH + HALO + (tt + 1) * 128],
                                 wv[:, ct * 1024 + dch * 512:ct * 1024 + (dch + 1) * 512],
                                 start=(ct == 0), stop=(ct == NCT - 1))
                stt.inc(i)
            ph.on("tensor", mmv)
            t_mmv = stt.n

            def evv(v, stg=stg, dch=dch, bank=bank, t_mmv=t_mmv, tt=tt):
                v.wait_ge(stt.sem, t_mmv)
                if dch == 0:
                    v.wait_ge(st.sem, max(vv_free[tt % 2], st_after_k if tt < 2 else 0))
                src = bank.ap().rearrange("p (h c) -> p h c", h=8)
                dst = stg.ap()[:, dch * 520:(dch + 1) * 520].rearrange("p (h c) -> p h c", c=65)[:, :, 0:64]
                sv.inc(v.tensor_copy(dst, src))
            ph.on("vector", evv)
            bank_v[id(bank)] = sv.n
        sv_vv = sv.n

        def svv(g, tt=tt, stg=stg, sv_vv=sv_vv):
            g.wait_ge(sv.sem, sv_vv)
            st.inc(g.dma_start(out=io["vvb"][tt * 128:(tt + 1) * 128, :], in_=stg[:]))
        ph.on("gpsimd", svv)
        vv_free[tt % 2] = st.n
    ph.free("vvs1", "vvs0", "wv")
    ph.free("vT")
    xrT = ph.sbuf("xrT", [128, NCT * (TC + 1)], BF16)
    xkT = ph.sbuf("xkT", [128, NCT * (TC + 1)], BF16)
    dxm = ph.sbuf("dxm", [128, TC + 1], BF16)
    for ct in range(NCT):
        def vmix(v, ct=ct):
            v.wait_ge(sv.sem, sv_vT_done)
            v.wait_ge(ld.sem, ld_const)
            base = ct * TCH
            D(v, v.tensor_tensor(dxm[:], xnT[:, base + 1:base + 2 + TC], xnT[:, base:base + 1 + TC], OP.subtract))
            v.scalar_tensor_tensor(xrT[:, ct * (TC + 1):(ct + 1) * (TC + 1)], dxm[:], mus[:, ct:ct + 1],
                                   xnT[:, base:base + 1 + TC], OP.mult, OP.add)
            sv.inc(v.scalar_tensor_tensor(xkT[:, ct * (TC + 1):(ct + 1) * (TC + 1)], dxm[:], mus[:, 8 + ct:9 + ct],
                                          xnT[:, base:base + 1 + TC], OP.mult, OP.add))
        ph.on("vector", vmix)
    sv_mix = sv.n
    ph.free("dxm")



    wr0 = ph.sbuf("wr0", [128, NCT * 1024], BF16)
    sg_wr0 = load_weight_bf("wr0_ld", io["WrT0"], wr0)
    wk0 = ph.sbuf("wk0", [128, NCT * 1024], BF16)
    sg_wk0 = load_weight_bf("wk0_ld", io["WkT0"], wk0)
    wv0 = ph.sbuf("wv0", [128, NCT * 1024], BF16)
    sg_wv0 = load_weight_bf("wv0_ld", io["WvT0"], wv0)
    r_ch = ph.sbuf("r_ch", [128, NCT * 512], BF16)
    ka_ch = ph.sbuf("ka_ch", [128, NCT * 512], BF16)
    krt = ph.sbuf("krt", [128, 512], BF16)
    obuf = [ph.sbuf(f"obuf{i}", [128, 512], F32) for i in range(2)]
    ob_free = {0: 0, 1: 0}
    kr_a = [0]
    r_v = {}
    ka_a = {}

    for (c0, n) in CH_TC1:
        for dt in range(NCT):
            bank = pmm[dt % 2]

            def mmr(t, dt=dt, c0=c0, n=n, bank=bank):
                t.wait_ge(sg.sem, sg_wv0)
                t.wait_ge(sv.sem, max(sv_mix, bank_v.get(id(bank), 0)))
                t.wait_ge(sa.sem, bank_a.get(id(bank), 0))
                i = None
                for ct in range(NCT):
                    i = t.matmul(bank[:, :n], wr0[:, ct * 1024 + dt * 128:ct * 1024 + (dt + 1) * 128],
                                 xrT[:, ct * (TC + 1) + c0:ct * (TC + 1) + c0 + n],
                                 start=(ct == 0), stop=(ct == NCT - 1))
                stt.inc(i)
            ph.on("tensor", mmr)
            t_mmr = stt.n

            def ear(a, dt=dt, n=n, bank=bank, t_mmr=t_mmr):
                a.wait_ge(stt.sem, t_mmr)
                a.wait_ge(sv.sem, r_v.get(dt, 0))
                sa.inc(a.activation(r_ch[:, dt * 512:dt * 512 + n], bank[:, :n], AF.Sigmoid))
            ph.on("scalar", ear)
            bank_a[id(bank)] = sa.n

        for dt in range(NCT):
            bank = pmm[dt % 2]

            def mmk(t, dt=dt, c0=c0, n=n, bank=bank):
                t.wait_ge(sg.sem, sg_wk0)
                t.wait_ge(sv.sem, bank_v.get(id(bank), 0))
                t.wait_ge(sa.sem, bank_a.get(id(bank), 0))
                i = None
                for ct in range(NCT):
                    i = t.matmul(bank[:, :n], wk0[:, ct * 1024 + dt * 128:ct * 1024 + (dt + 1) * 128],
                                 xkT[:, ct * (TC + 1) + c0:ct * (TC + 1) + c0 + n],
                                 start=(ct == 0), stop=(ct == NCT - 1))
                stt.inc(i)
            ph.on("tensor", mmk)
            t_mmk = stt.n

            def evk(v, n=n, bank=bank, t_mmk=t_mmk):
                v.wait_ge(stt.sem, t_mmk)
                v.wait_ge(sg.sem, sg_wv0)
                v.wait_ge(sa.sem, kr_a[0])
                sv.inc(v.tensor_scalar(krt[:, :n], bank[:, :n], 0.0, None, OP.max))
            ph.on("vector", evk)
            sv_kr = sv.n
            bank_v[id(bank)] = sv_kr

            def eak(a, dt=dt, n=n, sv_kr=sv_kr):
                a.wait_ge(sv.sem, sv_kr)
                a.wait_ge(stt.sem, ka_a.get(dt, 0))
                sa.inc(a.activation(ka_ch[:, dt * 512:dt * 512 + n], krt[:, :n], AF.Square))
            ph.on("scalar", eak)
            kr_a[0] = sa.n
        sa_ka_ch = sa.n

        for dt2 in range(NCT):
            bank = pmm[dt2 % 2]
            ob = obuf[dt2 % 2]

            def mmo(t, dt2=dt2, n=n, bank=bank, sa_ka_ch=sa_ka_ch):
                t.wait_ge(sg.sem, sg_wv0)
                t.wait_ge(sa.sem, max(sa_ka_ch, bank_a.get(id(bank), 0)))
                t.wait_ge(sv.sem, bank_v.get(id(bank), 0))
                i = None
                for ct in range(NCT):
                    i = t.matmul(bank[:, :n], wv0[:, ct * 1024 + dt2 * 128:ct * 1024 + (dt2 + 1) * 128],
                                 ka_ch[:, ct * 512:ct * 512 + n],
                                 start=(ct == 0), stop=(ct == NCT - 1))
                stt.inc(i)
            ph.on("tensor", mmo)
            t_mmo = stt.n

            def evo(v, dt2=dt2, n=n, bank=bank, ob=ob, t_mmo=t_mmo):
                v.wait_ge(stt.sem, t_mmo)
                v.wait_ge(st.sem, ob_free[dt2 % 2])
                sv.inc(v.tensor_tensor(ob[:, :n], r_ch[:, dt2 * 512:dt2 * 512 + n], bank[:, :n], OP.mult))
            ph.on("vector", evo)
            bank_v[id(bank)] = sv.n
            sv_ob = sv.n
            if dt2 == NCT - 1:
                for d_ in range(NCT):
                    r_v[d_] = sv.n
                    ka_a[d_] = stt.n

            def sfo(g, dt2=dt2, c0=c0, n=n, ob=ob, sv_ob=sv_ob):
                g.wait_ge(sv.sem, sv_ob)
                st.inc(g.dma_start(out=io["ffn0"][dt2 * 128:(dt2 + 1) * 128, c0:c0 + n], in_=ob[:, :n], allow_slow_non_contiguous=(n == 1)))
            ph.on("gpsimd", sfo)
            ob_free[dt2 % 2] = st.n

    def fin(g):
        g.wait_ge(st.sem, st.n)
    ph.on("gpsimd", fin)



def _hostT(w):
    return np.ascontiguousarray(np.asarray(w).T)

def _core_meta(i):
    return i // 4, i % 4, (i % 4) * TC

def _prep_phase0(inputs):
    x = np.asarray(inputs["L0_attn_x"])
    idb = np.eye(128, dtype=np.float32).astype(BF)
    hmask = np.zeros((128, 65), np.float32)
    hmask[1:64, 0] = 1.0
    hmask[65:128, 64] = 1.0
    hmask = hmask.astype(BF)
    common = dict(
        WqT=_hostT(inputs["L0_attn_Wq"]), WkT=_hostT(inputs["L0_attn_Wk"]),
        WvT=_hostT(inputs["L0_attn_Wv"]),
        WrT0=_hostT(inputs["L0_ffn_W_r"]), WkT0=_hostT(inputs["L0_ffn_W_k"]),
        WvT0=_hostT(inputs["L0_ffn_W_v"]),
        mu_r=np.ascontiguousarray(np.asarray(inputs["L0_ffn_mu_r"]).reshape(C, 1)),
        mu_k=np.ascontiguousarray(np.asarray(inputs["L0_ffn_mu_k"]).reshape(C, 1)),
        idb=idb, hmask=hmask,
    )
    per_core = []
    for i in range(8):
        b, j, t0 = _core_meta(i)
        xs = np.zeros((TCH, C), np.float32)
        lo = max(0, t0 - HALO)
        xs[HALO - (t0 - lo):] = x[b, lo:t0 + TC]
        per_core.append(dict(xs=np.ascontiguousarray(xs)))
    return common, per_core


def build_phase0():
    nc = bass.Bass("TRN2", target_bir_lowering=False)
    io = {}
    io["xs"] = nc.dram_tensor("xs", [TCH, C], F32, kind="ExternalInput").ap()
    for nm in ["WqT", "WkT", "WvT", "WrT0", "WkT0", "WvT0"]:
        io[nm] = nc.dram_tensor(nm, [C, C], F32, kind="ExternalInput").ap()
    io["mu_r"] = nc.dram_tensor("mu_r", [C, 1], F32, kind="ExternalInput").ap()
    io["mu_k"] = nc.dram_tensor("mu_k", [C, 1], F32, kind="ExternalInput").ap()
    io["idb"] = nc.dram_tensor("idb", [128, 128], BF16, kind="ExternalInput").ap()
    io["hmask"] = nc.dram_tensor("hmask", [128, 65], BF16, kind="ExternalInput").ap()
    io["qT"] = nc.dram_tensor("qT", [C, TCH], BF16, kind="ExternalOutput").ap()
    import os
    if os.environ.get("KDBG"):
        io["dbg_vT"] = nc.dram_tensor("dbg_vT", [C, TCH], BF16, kind="ExternalOutput").ap()
        io["dbg_xnT"] = nc.dram_tensor("dbg_xnT", [C, TCH], BF16, kind="ExternalOutput").ap()
    io["kb"] = nc.dram_tensor("kb", [16, C, 128], BF16, kind="ExternalOutput").ap()
    io["vvb"] = nc.dram_tensor("vvb", [TC, 1040], BF16, kind="ExternalOutput").ap()
    io["ffn0"] = nc.dram_tensor("ffn0", [C, TC + 1], F32, kind="ExternalOutput").ap()
    with ExitStack() as stack:
        ph = Ph(nc, stack)
        emit_phase0(ph, io)
    return nc


def run_phase0(inputs):
    common, per_core = _prep_phase0(inputs)
    nc = build_phase0()
    in_maps = [dict(common, **pc) for pc in per_core]
    return run_bass_kernel_spmd(nc, in_maps, core_ids=list(range(8)))


def kernel(**inputs):
    p0 = run_phase0(inputs).results
    p1 = run_phase1(inputs, p0).results
    x1s = [r["x1"] for r in p1]
    xn1s = [r["xn1"] for r in p1]
    a_res, b_res = run_phase2(inputs, x1s, xn1s)
    rkvs = [r["rwkv"] for r in b_res]
    p3 = run_phase3(inputs, x1s, xn1s, rkvs)
    out = np.zeros((B, T, C), np.float32)
    for i in range(8):
        b, j, t0 = _core_meta(i)
        out[b, t0:t0 + TC, :] = p3[i]["out"].T
    return out



NKT = 64
NBLK = 4
BLK = 16
QCH = [(2 + i * 512, 512) for i in range(4)]

def emit_phase1(ph, io):
    ld = ph.ctr("p1ld", 16)
    st = ph.ctr("p1st", 16)
    sv = ph.ctr("p1sv")
    sa = ph.ctr("p1sa")
    stt = ph.ctr("p1stt")
    sg = ph.ctr("p1sg")

    def D(v, i):
        sv.inc(i)
        v.wait_ge(sv.sem, sv.n)

    consts = ph.sbuf("c1", [128, 8], F32)
    onesrow = ph.sbuf("onesrow", [1, 128], BF16)
    onesf32 = ph.sbuf("onesf32", [128, 1], F32)
    m1f = ph.sbuf("m1f", [128, 1], BF16)
    ones128 = ph.sbuf("ones128", [128, 1], BF16)
    rrow = ph.sbuf("rrow", [1, 512], F32)
    rrbf = ph.sbuf("rrbf", [1, 512], BF16)

    def initc(v):
        v.memset(consts[:, 0:1], 1e-5)
        v.memset(consts[:, 1:2], 1.0 + 1e-6)
        v.memset(consts[:, 2:3], 1e-6)
        v.memset(onesrow[:], 1.0)
        v.memset(onesf32[:], 1.0)
        v.memset(ones128[:], 1.0)
        v.memset(m1f[:], 1.0)
        sv.inc(v.memset(m1f[0:1, :], 0.0))
    ph.on("vector", initc)
    c_ln = consts[:, 0:1]
    c_1p = consts[:, 1:2]
    c_e6 = consts[:, 2:3]

    q_s = ph.sbuf("q_s", [128, NCT * TCH], BF16)

    def loadq(e):
        for p in range(8):
            ld.inc(e.dma_start(out=q_s[:, p * TCH:(p + 1) * TCH], in_=io["qT"][p * 128:(p + 1) * 128, :]))
    ph.on("sync", loadq)
    ld_q = ld.n

    sacc = ph.sbuf("sacc", [128, 8 * TC], F32)
    lstage = ph.sbuf("lstage", [1, 8 * 512], F32)
    lbuf = ph.sbuf("lbuf", [1, 4 * 512], F32)
    kbuf = ph.sbuf("kbuf", [128, BLK * 1024], BF16)
    vbuf = ph.sbuf("vbuf", [128, BLK * 1040], BF16)
    p4 = ph.sbuf("p4", [128, 2048], BF16)
    pS = ph.psum("pS", [128, 2048], F32)
    pacc = [ph.psum(f"pacc{i}", [65, 512], F32) for i in range(4)]

    def zacc(v):
        sv.inc(v.memset(sacc[:], 0.0))
    ph.on("vector", zacc)
    sv_zacc = sv.n
    lst_free = {s: 0 for s in range(8)}
    lslot = [0]

    slot_ld = {}
    slot_free = {}
    exp_done = [0]
    evac_done = {i: 0 for i in range(4)}

    for blk in range(NBLK):
        for s in range(BLK):
            kt = blk * BLK + s

            def loadkv(e, kt=kt, s=s, blk=blk):
                if blk > 0:
                    e.wait_ge(stt.sem, slot_free.get(s, 0))
                kin = io["kb_full"][kt].rearrange("(ct p) n -> p ct n", p=128)
                ld.inc(e.dma_start(out=kbuf.ap()[:, s * 1024:(s + 1) * 1024].rearrange("p (ct n) -> p ct n", ct=8), in_=kin))
                ld.inc(e.dma_start(out=vbuf[:, s * 1040:(s + 1) * 1040], in_=io["vv_full"][kt * 128:(kt + 1) * 128, :]))
            ph.on("sync", loadkv)
            slot_ld[s] = ld.n

        for quad in range(4):
            for (c0, n) in QCH:
                for s in range(BLK):
                    def smm(t, quad=quad, s=s, c0=c0, n=n, first=(s == 0)):
                        t.wait_ge(ld.sem, max(slot_ld[s], ld_q))
                        t.wait_ge(sa.sem, exp_done[0])
                        if first:
                            for i2 in range(4):
                                t.wait_ge(sv.sem, evac_done[i2])
                        i = None
                        for pp in range(2):
                            p = 2 * quad + pp
                            kb = kbuf[:, s * 1024 + p * 128:s * 1024 + (p + 1) * 128]
                            qb = q_s[:, p * TCH + c0:p * TCH + c0 + n]
                            t.matmul(pS[:, (2 * pp) * 512:(2 * pp) * 512 + n], kb[0:64, :], qb[0:64, :],
                                     start=True, stop=True, tile_position=(0, 0))
                            i = t.matmul(pS[:, (2 * pp + 1) * 512:(2 * pp + 1) * 512 + n], kb[64:128, :], qb[64:128, :],
                                         start=True, stop=True, tile_position=(64, 0))
                        stt.inc(i)
                    ph.on("tensor", smm)
                    t_s = stt.n

                    def expq(a, t_s=t_s):
                        a.wait_ge(stt.sem, t_s)
                        sa.inc(a.activation(p4[:, :], pS[:, :], AF.Exp, scale=0.125))
                    ph.on("scalar", expq)
                    exp_done[0] = sa.n
                    t_exp = sa.n

                    def avmm(t, quad=quad, s=s, n=n, t_exp=t_exp):
                        t.wait_ge(sa.sem, t_exp)
                        i = None
                        for hh in range(4):
                            h = 4 * quad + hh
                            vb = vbuf[:, s * 1040 + h * 65:s * 1040 + (h + 1) * 65]
                            i = t.matmul(pacc[hh][:, :n], vb, p4[:, hh * 512:hh * 512 + n],
                                         start=(s == 0), stop=(s == BLK - 1))
                        stt.inc(i)
                    ph.on("tensor", avmm)
                t_av = stt.n

                ci = (c0 - 2) // 512
                for hh in range(4):
                    h = 4 * quad + hh
                    p = h // 2
                    r0 = (h % 2) * 64

                    slot = lslot[0] % 8
                    lslot[0] += 1

                    def evac(v, hh=hh, h=h, p=p, r0=r0, c0=c0, n=n, t_av=t_av, slot=slot):
                        v.wait_ge(stt.sem, t_av)
                        v.wait_ge(sv.sem, sv_zacc)
                        v.wait_ge(st.sem, lst_free[slot])
                        dst = sacc[r0:r0 + 64, p * TC + (c0 - 2):p * TC + (c0 - 2) + n]
                        v.tensor_tensor(dst, dst, pacc[hh][0:64, :n], OP.add)
                        sv.inc(v.tensor_copy(lstage[0:1, slot * 512:slot * 512 + n], pacc[hh][64:65, :n]))
                        evac_done[hh] = sv.n
                    ph.on("vector", evac)
                    sv_ev = sv.n

                    def stlp(g, blk=blk, h=h, ci=ci, n=n, sv_ev=sv_ev, slot=slot):
                        g.wait_ge(sv.sem, sv_ev)
                        st.inc(g.dma_start(out=io["lpart"][blk, h, ci, :n], in_=lstage[0:1, slot * 512:slot * 512 + n]))
                    ph.on("gpsimd", stlp)
                    lst_free[slot] = st.n

        for s in range(BLK):
            slot_free[s] = stt.n

    ph.free("pacc3", "pacc2", "pacc1", "pacc0", "pS")
    pbc = ph.psum("pbc", [128, 512], F32)
    sv_attn_done = sv.n
    bc_free = [0]
    lb_free = [0]
    for p in range(8):
        for (c0, n) in QCH:
            for half in range(2):
                h = 2 * p + half
                r0 = half * 64

                ldl = ld.n

                def ldlp(e, h=h, c0=c0, n=n):
                    e.wait_ge(sv.sem, lb_free[0])
                    ld.inc(e.dma_start(out=lbuf[0:1, :].rearrange("p (b n) -> p b n", b=4),
                                       in_=io["lpart"][:, h, (c0 - 2) // 512, :]))
                ph.on("sync", ldlp)
                ldl = ld.n

                def rdiv(v, h=h, c0=c0, n=n, ldl=ldl):
                    v.wait_ge(sv.sem, sv_attn_done)
                    v.wait_ge(ld.sem, ldl)
                    D(v, v.tensor_tensor(rrow[0:1, :n], lbuf[0:1, 0:n], lbuf[0:1, 512:512 + n], OP.add))
                    D(v, v.tensor_tensor(rrow[0:1, :n], rrow[0:1, :n], lbuf[0:1, 1024:1024 + n], OP.add))
                    D(v, v.tensor_tensor(rrow[0:1, :n], rrow[0:1, :n], lbuf[0:1, 1536:1536 + n], OP.add))
                    lb_free[0] = sv.n
                    D(v, v.reciprocal(rrow[0:1, :n], rrow[0:1, :n]))
                    sv.inc(v.tensor_copy(rrbf[0:1, :n], rrow[0:1, :n]))
                ph.on("vector", rdiv)
                sv_r = sv.n

                def bcmm(t, n=n, sv_r=sv_r):
                    t.wait_ge(sv.sem, max(sv_r, bc_free[0]))
                    stt.inc(t.matmul(pbc[0:64, :n], onesrow[0:1, 0:64], rrbf[0:1, :n], start=True, stop=True))
                ph.on("tensor", bcmm)
                t_bc = stt.n

                def omul(v, p=p, r0=r0, c0=c0, n=n, t_bc=t_bc):
                    v.wait_ge(stt.sem, t_bc)
                    sv.inc(v.tensor_tensor(q_s[r0:r0 + 64, p * TCH + c0:p * TCH + c0 + n],
                                           sacc[r0:r0 + 64, p * TC + (c0 - 2):p * TC + (c0 - 2) + n],
                                           pbc[0:64, :n], OP.mult))
                ph.on("vector", omul)
                bc_free[0] = sv.n
    sv_oT = sv.n

    ph.free("p4", "vbuf", "kbuf", "lbuf", "lstage", "sacc")

    def load_weight_bf(name, dram, wbuf, kdim=1024):
        nt = kdim // 128
        wf = ph.sbuf(name + "_f", [128, 1024], F32)
        snap = (sv.n, sa.n, stt.n, st.n)
        for ctk in range(nt):
            prev_cast = sg.n

            def loadf(e, ctk=ctk, prev_cast=prev_cast, snap=snap):
                if ctk == 0:
                    e.wait_ge(sv.sem, snap[0])
                    e.wait_ge(sa.sem, snap[1])
                    e.wait_ge(stt.sem, snap[2])
                    e.wait_ge(st.sem, snap[3])
                e.wait_ge(sg.sem, prev_cast)
                ld.inc(e.dma_start(out=wf[:], in_=dram[ctk * 128:(ctk + 1) * 128, :]))
            ph.on("sync", loadf)
            ldv = ld.n

            def cast(g, ctk=ctk, ldv=ldv):
                g.wait_ge(ld.sem, ldv)
                sg.inc(g.tensor_copy(wbuf[:, ctk * 1024:(ctk + 1) * 1024], wf[:]))
            ph.on("gpsimd", cast)
        ph.free(name + "_f")
        return sg.n

    wo = ph.sbuf("wo", [128, NCT * 1024], BF16)
    sg_wo = load_weight_bf("wo_ld", io["WoT"], wo)

    wob = [ph.sbuf(f"wob{i}", [128, NCT * 512], F32) for i in range(2)]
    a0b = [ph.sbuf(f"a0b{i}", [128, NCT * 512], F32) for i in range(2)]
    x1b = [ph.sbuf(f"x1b{i}", [128, NCT * 512], F32) for i in range(2)]
    sqw = ph.sbuf("sqw", [128, 512], BF16)
    rows = ph.sbuf("rows", [1, 16 * 512], F32)
    rwb = ph.sbuf("rwb", [1, 512], BF16)
    xsb = [ph.sbuf(f"xsb{i}", [128, 512], F32) for i in range(2)]
    ffb = [ph.sbuf(f"ffb{i}", [128, 512], F32) for i in range(2)]
    x1sq = ph.sbuf("x1sq", [128, 512], F32)
    xnst = [ph.sbuf(f"xnst{i}", [128, 512], F32) for i in range(2)]
    pmm = [ph.psum(f"p1mm{i}", [128, 512], F32) for i in range(2)]
    pnr = ph.psum("pnr", [1, 512], F32)
    pnr2 = ph.psum("pnr2", [1, 512], F32)
    pbc2 = ph.psum("pbc2", [128, 512], F32)
    pbc3 = ph.psum("pbc3", [128, 512], F32)

    bank_v = {}
    sq_guard = [0]
    x1sq_guard = [0]
    pnr_v = [0]
    pnr_a = [0]
    pnr2_v = [0]
    res_free = {0: 0, 1: 0}
    xn_free = {0: 0, 1: 0}
    chunk_users = {0: (0, 0, 0), 1: (0, 0, 0)}

    for ci, (c0, n) in enumerate(_chunks(TC)):
        par = ci % 2
        wb = wob[par]
        ab = a0b[par]
        xb = x1b[par]

        for dt in range(NCT):
            bank = pmm[dt % 2]

            def mmwo(t, dt=dt, c0=c0, n=n, bank=bank):
                t.wait_ge(sg.sem, sg_wo)
                t.wait_ge(sv.sem, max(sv_oT, bank_v.get(id(bank), 0)))
                i = None
                for ct in range(NCT):
                    i = t.matmul(bank[:, :n], wo[:, ct * 1024 + dt * 128:ct * 1024 + (dt + 1) * 128],
                                 q_s[:, ct * TCH + 2 + c0:ct * TCH + 2 + c0 + n],
                                 start=(ct == 0), stop=(ct == NCT - 1))
                stt.inc(i)
            ph.on("tensor", mmwo)
            t_wo = stt.n

            def evwo(v, dt=dt, n=n, bank=bank, wb=wb, t_wo=t_wo, ci=ci, par=par):
                v.wait_ge(stt.sem, t_wo)
                if ci >= 2 and dt == 0:
                    v.wait_ge(sv.sem, chunk_users[par][0])
                    v.wait_ge(sa.sem, chunk_users[par][1])
                    v.wait_ge(st.sem, chunk_users[par][2])
                sv.inc(v.tensor_copy(wb[:, dt * 512:dt * 512 + n], bank[:, :n]))
            ph.on("vector", evwo)
            sv_wo = sv.n
            bank_v[id(bank)] = sv_wo

            def sqwo(a, dt=dt, n=n, wb=wb, sv_wo=sv_wo):
                a.wait_ge(sv.sem, sv_wo)
                a.wait_ge(stt.sem, sq_guard[0])
                sa.inc(a.activation(sqw[:, :n], wb[:, dt * 512:dt * 512 + n], AF.Square))
            ph.on("scalar", sqwo)
            sa_sqw = sa.n

            def nmm(t, dt=dt, n=n, sa_sqw=sa_sqw):
                t.wait_ge(sa.sem, sa_sqw)
                if dt == 0:
                    t.wait_ge(sv.sem, pnr_v[0])
                    t.wait_ge(sa.sem, pnr_a[0])
                stt.inc(t.matmul(pnr[0:1, :n], m1f[:, 0:1] if dt == 0 else ones128[:, 0:1], sqw[:, :n],
                                 start=(dt == 0), stop=(dt == NCT - 1)))
            ph.on("tensor", nmm)
            sq_guard[0] = stt.n
        t_nr = stt.n

        def rops1(a, n=n, t_nr=t_nr):
            a.wait_ge(stt.sem, t_nr)
            sa.inc(a.activation(rows[0:1, 0:n], pnr[0:1, :n], AF.Sqrt, bias=c_e6[0:1]))
        ph.on("scalar", rops1)
        sa_vn0 = sa.n
        pnr_a[0] = sa_vn0

        def rops2(v, n=n, sa_vn0=sa_vn0):
            v.wait_ge(sa.sem, sa_vn0)
            sv.inc(v.tensor_scalar(rows[0:1, 512:512 + n], rows[0:1, 0:n], 5.0, None, OP.min))
        ph.on("vector", rops2)
        sv_vn = sv.n

        def rops3(a, n=n, sv_vn=sv_vn):
            a.wait_ge(sv.sem, sv_vn)
            a.activation(rows[0:1, 1024:1024 + n], rows[0:1, 512:512 + n], AF.Exp)
            sa.inc(a.activation(rows[0:1, 1536:1536 + n], rows[0:1, 512:512 + n], AF.Exp, scale=-1.0))
        ph.on("scalar", rops3)
        sa_ee = sa.n

        def rops4(v, n=n, sa_ee=sa_ee):
            v.wait_ge(sa.sem, sa_ee)
            D(v, v.tensor_tensor(rows[0:1, 2048:2048 + n], rows[0:1, 1024:1024 + n], rows[0:1, 1536:1536 + n], OP.subtract))
            D(v, v.tensor_scalar(rows[0:1, 2048:2048 + n], rows[0:1, 2048:2048 + n], 0.5, None, OP.mult))
            D(v, v.tensor_scalar(rows[0:1, 2560:2560 + n], rows[0:1, 512:512 + n], 1e-6, None, OP.add))
            D(v, v.reciprocal(rows[0:1, 3072:3072 + n], rows[0:1, 2560:2560 + n]))
            D(v, v.tensor_tensor(rows[0:1, 3584:3584 + n], rows[0:1, 2048:2048 + n], rows[0:1, 3072:3072 + n], OP.mult))
            sv.inc(v.tensor_copy(rwb[0:1, :n], rows[0:1, 3584:3584 + n]))
        ph.on("vector", rops4)
        sv_sd = sv.n

        def bc2(t, n=n, sv_sd=sv_sd):
            t.wait_ge(sv.sem, max(sv_sd, bank_v.get(id(pbc2), 0)))
            stt.inc(t.matmul(pbc2[:, :n], onesrow[0:1, :], rwb[0:1, :n], start=True, stop=True))
        ph.on("tensor", bc2)
        t_bc2 = stt.n

        for dt in range(NCT):
            def mka(v, dt=dt, n=n, wb=wb, ab=ab, t_bc2=t_bc2):
                v.wait_ge(stt.sem, t_bc2)
                D(v, v.tensor_tensor(ab[:, dt * 512:dt * 512 + n], wb[:, dt * 512:dt * 512 + n], pbc2[:, :n], OP.mult))
                sv.inc(v.tensor_scalar(ab[:, dt * 512:dt * 512 + n], ab[:, dt * 512:dt * 512 + n], -8.0, 8.0, OP.max, OP.min))
            ph.on("vector", mka)
            sv_a0 = sv.n
            bank_v[id(pbc2)] = sv_a0

            def sqa(a, dt=dt, n=n, ab=ab, sv_a0=sv_a0):
                a.wait_ge(sv.sem, sv_a0)
                a.wait_ge(stt.sem, sq_guard[0])
                sa.inc(a.activation(sqw[:, :n], ab[:, dt * 512:dt * 512 + n], AF.Square))
            ph.on("scalar", sqa)
            sa_sqa = sa.n

            def nmm2(t, dt=dt, n=n, sa_sqa=sa_sqa):
                t.wait_ge(sa.sem, sa_sqa)
                if dt == 0:
                    t.wait_ge(sa.sem, pnr_a[0])
                stt.inc(t.matmul(pnr[0:1, :n], m1f[:, 0:1] if dt == 0 else ones128[:, 0:1], sqw[:, :n],
                                 start=(dt == 0), stop=(dt == NCT - 1)))
            ph.on("tensor", nmm2)
            sq_guard[0] = stt.n
        t_nr2 = stt.n

        def x0r(a, n=n, t_nr2=t_nr2):
            a.wait_ge(stt.sem, t_nr2)
            sa.inc(a.activation(rows[0:1, 4096:4096 + n], pnr[0:1, :n], AF.Sqrt, bias=c_1p[0:1]))
        ph.on("scalar", x0r)
        sa_x0r = sa.n
        pnr_a[0] = sa_x0r

        def wx0(v, n=n, ab=ab, sa_x0r=sa_x0r):
            v.wait_ge(sa.sem, sa_x0r)
            sv.inc(v.tensor_copy(ab[0:1, 0:n], rows[0:1, 4096:4096 + n]))
        ph.on("vector", wx0)
        sv_x0w = sv.n

        for dt in range(NCT):
            xs_t = xsb[dt % 2]
            ff_t = ffb[dt % 2]

            def ldres(e, dt=dt, c0=c0, n=n, xs_t=xs_t, ff_t=ff_t):
                e.wait_ge(sv.sem, res_free[dt % 2])
                ld.inc(e.dma_start(out=xs_t[:, :n], in_=io["xsT"][dt * 128:(dt + 1) * 128, 2 + c0:2 + c0 + n]))
                ld.inc(e.dma_start(out=ff_t[:, :n], in_=io["ffn0"][dt * 128:(dt + 1) * 128, 1 + c0:1 + c0 + n]))
            ph.on("sync", ldres)
            ld_res = ld.n

            def mkx1(v, dt=dt, n=n, ab=ab, xb=xb, xs_t=xs_t, ff_t=ff_t, ld_res=ld_res, sv_x0w=sv_x0w):
                v.wait_ge(ld.sem, ld_res)
                v.wait_ge(sv.sem, sv_x0w)
                D(v, v.tensor_tensor(xb[:, dt * 512:dt * 512 + n], xs_t[:, :n], ab[:, dt * 512:dt * 512 + n], OP.add))
                sv.inc(v.tensor_tensor(xb[:, dt * 512:dt * 512 + n], xb[:, dt * 512:dt * 512 + n], ff_t[:, :n], OP.add))
                res_free[dt % 2] = sv.n
            ph.on("vector", mkx1)
            sv_x1 = sv.n

            def stx1(g, dt=dt, c0=c0, n=n, xb=xb, sv_x1=sv_x1):
                g.wait_ge(sv.sem, sv_x1)
                st.inc(g.dma_start(out=io["x1"][dt * 128:(dt + 1) * 128, c0:c0 + n], in_=xb[:, dt * 512:dt * 512 + n]))
            ph.on("gpsimd", stx1)

            def smm1(t, dt=dt, n=n, xb=xb, sv_x1=sv_x1):
                t.wait_ge(sv.sem, sv_x1)
                if dt == 0:
                    t.wait_ge(sv.sem, pnr_v[0])
                    t.wait_ge(sa.sem, pnr_a[0])
                stt.inc(t.matmul(pnr[0:1, :n], onesf32[:, 0:1], xb[:, dt * 512:dt * 512 + n],
                                 start=(dt == 0), stop=(dt == NCT - 1)))
            ph.on("tensor", smm1)

            def sqx1(a, dt=dt, n=n, xb=xb, sv_x1=sv_x1):
                a.wait_ge(sv.sem, sv_x1)
                a.wait_ge(stt.sem, x1sq_guard[0])
                sa.inc(a.activation(x1sq[:, :n], xb[:, dt * 512:dt * 512 + n], AF.Square))
            ph.on("scalar", sqx1)
            sa_sqx = sa.n

            def smm2(t, dt=dt, n=n, sa_sqx=sa_sqx):
                t.wait_ge(sa.sem, sa_sqx)
                if dt == 0:
                    t.wait_ge(sv.sem, pnr2_v[0])
                stt.inc(t.matmul(pnr2[0:1, :n], onesf32[:, 0:1], x1sq[:, :n],
                                 start=(dt == 0), stop=(dt == NCT - 1)))
            ph.on("tensor", smm2)
            x1sq_guard[0] = stt.n
        t_stats = stt.n

        def lnr1(v, n=n, t_stats=t_stats):
            v.wait_ge(stt.sem, t_stats)
            D(v, v.tensor_scalar(rows[0:1, 4608:4608 + n], pnr[0:1, :n], 1.0 / C, None, OP.mult))
            D(v, v.tensor_scalar(rows[0:1, 5120:5120 + n], pnr2[0:1, :n], 1.0 / C, None, OP.mult))
            D(v, v.tensor_tensor(rows[0:1, 5632:5632 + n], rows[0:1, 4608:4608 + n], rows[0:1, 4608:4608 + n], OP.mult))
            sv.inc(v.tensor_tensor(rows[0:1, 5632:5632 + n], rows[0:1, 5120:5120 + n], rows[0:1, 5632:5632 + n], OP.subtract))
        ph.on("vector", lnr1)
        sv_var1 = sv.n
        pnr_v[0] = sv_var1
        pnr2_v[0] = sv_var1

        def lnr2(a, n=n, sv_var1=sv_var1):
            a.wait_ge(sv.sem, sv_var1)
            sa.inc(a.activation(rows[0:1, 6144:6144 + n], rows[0:1, 5632:5632 + n], AF.Sqrt, bias=c_ln[0:1]))
        ph.on("scalar", lnr2)
        sa_sd1 = sa.n

        def lnr3(v, n=n, sa_sd1=sa_sd1):
            v.wait_ge(sa.sem, sa_sd1)
            D(v, v.reciprocal(rows[0:1, 6656:6656 + n], rows[0:1, 6144:6144 + n]))
            D(v, v.tensor_tensor(rows[0:1, 7168:7168 + n], rows[0:1, 4608:4608 + n], rows[0:1, 6656:6656 + n], OP.mult))
            sv.inc(v.tensor_copy(rwb[0:1, :n], rows[0:1, 6656:6656 + n]))
        ph.on("vector", lnr3)
        sv_rstd = sv.n

        def bc3(t, n=n, sv_rstd=sv_rstd):
            t.wait_ge(sv.sem, max(sv_rstd, bank_v.get(id(pbc2), 0)))
            stt.inc(t.matmul(pbc2[:, :n], onesrow[0:1, :], rwb[0:1, :n], start=True, stop=True))
        ph.on("tensor", bc3)
        t_bc3 = stt.n

        def mrow(v, n=n, t_bc3=t_bc3):
            v.wait_ge(stt.sem, t_bc3)
            sv.inc(v.tensor_copy(rwb[0:1, :n], rows[0:1, 7168:7168 + n]))
        ph.on("vector", mrow)
        sv_mr = sv.n

        def bc4(t, n=n, sv_mr=sv_mr):
            t.wait_ge(sv.sem, max(sv_mr, bank_v.get(id(pbc3), 0)))
            stt.inc(t.matmul(pbc3[:, :n], onesrow[0:1, :], rwb[0:1, :n], start=True, stop=True))
        ph.on("tensor", bc4)
        t_bc4 = stt.n

        for dt in range(NCT):
            xn_t = xnst[dt % 2]

            def mkxn1(v, dt=dt, n=n, xb=xb, xn_t=xn_t, t_bc4=t_bc4):
                v.wait_ge(stt.sem, t_bc4)
                v.wait_ge(st.sem, xn_free[dt % 2])
                D(v, v.tensor_tensor(xn_t[:, :n], xb[:, dt * 512:dt * 512 + n], pbc2[:, :n], OP.mult))
                sv.inc(v.tensor_tensor(xn_t[:, :n], xn_t[:, :n], pbc3[:, :n], OP.subtract))
            ph.on("vector", mkxn1)
            sv_xn1 = sv.n
            bank_v[id(pbc2)] = sv_xn1
            bank_v[id(pbc3)] = sv_xn1

            def stxn1(g, dt=dt, c0=c0, n=n, xn_t=xn_t, sv_xn1=sv_xn1):
                g.wait_ge(sv.sem, sv_xn1)
                st.inc(g.dma_start(out=io["xn1"][dt * 128:(dt + 1) * 128, c0:c0 + n], in_=xn_t[:, :n]))
            ph.on("gpsimd", stxn1)
            xn_free[dt % 2] = st.n
        chunk_users[par] = (sv.n, sa.n, st.n)

    def fin(g):
        g.wait_ge(st.sem, st.n)
    ph.on("gpsimd", fin)


def build_phase1():
    nc = bass.Bass("TRN2", target_bir_lowering=False)
    io = {}
    io["qT"] = nc.dram_tensor("qT", [C, TCH], BF16, kind="ExternalInput").ap()
    io["kb_full"] = nc.dram_tensor("kb_full", [NKT, C, 128], BF16, kind="ExternalInput").ap()
    io["vv_full"] = nc.dram_tensor("vv_full", [T, 1040], BF16, kind="ExternalInput").ap()
    io["WoT"] = nc.dram_tensor("WoT", [C, C], F32, kind="ExternalInput").ap()
    io["xsT"] = nc.dram_tensor("xsT", [C, TCH], F32, kind="ExternalInput").ap()
    io["ffn0"] = nc.dram_tensor("ffn0", [C, TC + 1], F32, kind="ExternalInput").ap()
    io["x1"] = nc.dram_tensor("x1", [C, TC], F32, kind="ExternalOutput").ap()
    io["xn1"] = nc.dram_tensor("xn1", [C, TC], F32, kind="ExternalOutput").ap()
    io["lpart"] = nc.dram_tensor("lpart", [NBLK, 16, 4, 512], F32).ap()
    with ExitStack() as stack:
        ph = Ph(nc, stack)
        emit_phase1(ph, io)
    return nc


def _prep_phase1(inputs, p0res):
    x = np.asarray(inputs["L0_attn_x"])
    common = dict(WoT=_hostT(inputs["L0_attn_Wo"]))
    per_core = []
    for i in range(8):
        b, j, t0 = _core_meta(i)
        kb_full = np.concatenate([p0res[4 * b + jj]["kb"] for jj in range(4)], axis=0)
        vv_full = np.concatenate([p0res[4 * b + jj]["vvb"] for jj in range(4)], axis=0)
        xs = np.zeros((TCH, C), np.float32)
        lo = max(0, t0 - HALO)
        xs[HALO - (t0 - lo):] = x[b, lo:t0 + TC]
        per_core.append(dict(
            qT=p0res[i]["qT"], kb_full=np.ascontiguousarray(kb_full),
            vv_full=np.ascontiguousarray(vv_full),
            xsT=np.ascontiguousarray(xs.T), ffn0=p0res[i]["ffn0"],
        ))
    return common, per_core


def run_phase1(inputs, p0res):
    common, per_core = _prep_phase1(inputs, p0res)
    nc = build_phase1()
    in_maps = [dict(common, **pc) for pc in per_core]
    return run_bass_kernel_spmd(nc, in_maps, core_ids=list(range(8)))



CH_X = [(0, 512), (512, 512), (1024, 512), (1536, 512), (2048, 1)]
CH_T = [(0, 512), (512, 512), (1024, 512), (1536, 512)]

def _emit_ln_feature_major(ph, ctrs, xbuf, xw, nchunks, helpers):
    pass


def emit_phase2a(ph, io):
    ld = ph.ctr("ald", 16)
    st = ph.ctr("ast", 16)
    sv = ph.ctr("asv")
    sa = ph.ctr("asa")
    stt = ph.ctr("astt")
    sg = ph.ctr("asg")

    def D(v, i):
        sv.inc(i)
        v.wait_ge(sv.sem, sv.n)

    consts = ph.sbuf("c2", [128, 4], F32)
    onesb1 = ph.sbuf("onesb1", [128, 1], BF16)
    onesrow = ph.sbuf("onesrow", [1, 128], BF16)
    onestb = ph.sbuf("onestb", [128, TC], BF16)
    pv8 = ph.sbuf("pv8", [128, 8 * 8], F32)

    def initc(v):
        v.memset(consts[:, 0:1], 1e-5)
        v.memset(onesb1[:], 1.0)
        v.memset(onesrow[:], 1.0)
        sv.inc(v.memset(onestb[:], 1.0))
    ph.on("vector", initc)
    c_ln = consts[:, 0:1]

    def loadp(e):
        names = ["ln1_g", "ln1_b", "amk", "amv", "amr", "td", "tf", "hm"]
        for ci, nm in enumerate(names):
            for ct in range(NCT):
                ld.inc(e.dma_start(out=pv8[:, ct * 8 + ci:ct * 8 + ci + 1],
                                   in_=io[nm][ct * 128:(ct + 1) * 128, :]))
    ph.on("sync", loadp)
    ld_p = ld.n

    xkm = ph.sbuf("xkm", [128, NCT * TC], BF16)
    xvm = ph.sbuf("xvm", [128, NCT * TC], BF16)
    xrm = ph.sbuf("xrm", [128, NCT * TC], BF16)

    xln1 = ph.sbuf("xln1", [128, NCT * (TC + 1)], BF16)
    rows = ph.sbuf("rows", [1, 8 * 512], F32)
    rwb = ph.sbuf("rwb", [1, 512], BF16)
    xn1h = ph.sbuf("s_xn1h", [128, NCT * (TC + 1)], BF16)
    xldf = ph.sbuf("xldf", [128, TC + 1], F32)
    sqx = ph.sbuf("sqx", [128, 512], F32)
    sqb2 = ph.sbuf("sqb2", [128, 512], BF16)

    cast_done = 0
    for ct in range(NCT):
        prev = cast_done

        def ldt(e, ct=ct, prev=prev):
            e.wait_ge(sv.sem, prev)
            ld.inc(e.dma_start(out=xldf[:], in_=io["xn1h"][ct * 128:(ct + 1) * 128, :]))
        ph.on("sync", ldt)
        ldv = ld.n

        def cst(v, ct=ct, ldv=ldv):
            v.wait_ge(ld.sem, ldv)
            sv.inc(v.tensor_copy(xn1h[:, ct * (TC + 1):(ct + 1) * (TC + 1)], xldf[:]))
        ph.on("vector", cst)
        cast_done = sv.n
    ld_x = cast_done

    pnr = ph.psum("pnr", [1, 512], F32)
    pnr2 = ph.psum("pnr2", [1, 512], F32)
    pbc2 = ph.psum("pbc2", [128, 512], F32)
    pbc3 = ph.psum("pbc3", [128, 512], F32)
    pnr_v = [0]
    pnr_a = [0]
    pnr2_v = [0]
    sq_g = [0]
    pbc_v = {}

    for (c0, n) in CH_X:
        for dt in range(NCT):
            def smm1(t, dt=dt, c0=c0, n=n):
                t.wait_ge(sv.sem, ld_x)
                if dt == 0:
                    t.wait_ge(sv.sem, pnr_v[0])
                    t.wait_ge(sa.sem, pnr_a[0])
                stt.inc(t.matmul(pnr[0:1, :n], onesb1[:, 0:1], xn1h[:, dt * (TC + 1) + c0:dt * (TC + 1) + c0 + n],
                                 start=(dt == 0), stop=(dt == NCT - 1)))
            ph.on("tensor", smm1)

            def sqx1(a, dt=dt, c0=c0, n=n):
                a.wait_ge(sv.sem, ld_x)
                a.wait_ge(stt.sem, sq_g[0])
                sa.inc(a.activation(sqb2[:, :n], xn1h[:, dt * (TC + 1) + c0:dt * (TC + 1) + c0 + n], AF.Square))
            ph.on("scalar", sqx1)
            sa_s = sa.n

            def smm2(t, dt=dt, n=n, sa_s=sa_s):
                t.wait_ge(sa.sem, sa_s)
                if dt == 0:
                    t.wait_ge(sv.sem, pnr2_v[0])
                stt.inc(t.matmul(pnr2[0:1, :n], onesb1[:, 0:1], sqb2[:, :n],
                                 start=(dt == 0), stop=(dt == NCT - 1)))
            ph.on("tensor", smm2)
            sq_g[0] = stt.n
        t_st = stt.n

        def lnr1(v, n=n, t_st=t_st):
            v.wait_ge(stt.sem, t_st)
            D(v, v.tensor_scalar(rows[0:1, 0:n], pnr[0:1, :n], 1.0 / C, None, OP.mult))
            D(v, v.tensor_scalar(rows[0:1, 512:512 + n], pnr2[0:1, :n], 1.0 / C, None, OP.mult))
            D(v, v.tensor_tensor(rows[0:1, 1024:1024 + n], rows[0:1, 0:n], rows[0:1, 0:n], OP.mult))
            sv.inc(v.tensor_tensor(rows[0:1, 1024:1024 + n], rows[0:1, 512:512 + n], rows[0:1, 1024:1024 + n], OP.subtract))
        ph.on("vector", lnr1)
        sv_var = sv.n
        pnr_v[0] = sv_var
        pnr2_v[0] = sv_var

        def lnr2(a, n=n, sv_var=sv_var):
            a.wait_ge(sv.sem, sv_var)
            sa.inc(a.activation(rows[0:1, 1536:1536 + n], rows[0:1, 1024:1024 + n], AF.Sqrt, bias=c_ln[0:1]))
        ph.on("scalar", lnr2)
        sa_sd = sa.n
        pnr_a[0] = sa_sd

        def lnr3(v, n=n, sa_sd=sa_sd):
            v.wait_ge(sa.sem, sa_sd)
            D(v, v.reciprocal(rows[0:1, 2048:2048 + n], rows[0:1, 1536:1536 + n]))
            D(v, v.tensor_tensor(rows[0:1, 2560:2560 + n], rows[0:1, 0:n], rows[0:1, 2048:2048 + n], OP.mult))
            sv.inc(v.tensor_copy(rwb[0:1, :n], rows[0:1, 2048:2048 + n]))
        ph.on("vector", lnr3)
        sv_r = sv.n

        def bcr(t, n=n, sv_r=sv_r):
            t.wait_ge(sv.sem, max(sv_r, pbc_v.get(id(pbc2), 0)))
            stt.inc(t.matmul(pbc2[:, :n], onesrow[0:1, :], rwb[0:1, :n], start=True, stop=True))
        ph.on("tensor", bcr)
        t_b2 = stt.n

        def mr(v, n=n, t_b2=t_b2):
            v.wait_ge(stt.sem, t_b2)
            sv.inc(v.tensor_copy(rwb[0:1, :n], rows[0:1, 2560:2560 + n]))
        ph.on("vector", mr)
        sv_mr = sv.n

        def bcm(t, n=n, sv_mr=sv_mr):
            t.wait_ge(sv.sem, max(sv_mr, pbc_v.get(id(pbc3), 0)))
            stt.inc(t.matmul(pbc3[:, :n], onesrow[0:1, :], rwb[0:1, :n], start=True, stop=True))
        ph.on("tensor", bcm)
        t_b3 = stt.n

        for dt in range(NCT):
            def mkln(v, dt=dt, c0=c0, n=n, t_b3=t_b3):
                v.wait_ge(stt.sem, t_b3)
                v.wait_ge(ld.sem, ld_p)
                D(v, v.tensor_tensor(sqx[:, :n], xn1h[:, dt * (TC + 1) + c0:dt * (TC + 1) + c0 + n], pbc2[:, :n], OP.mult))
                D(v, v.tensor_tensor(sqx[:, :n], sqx[:, :n], pbc3[:, :n], OP.subtract))
                sv.inc(v.tensor_scalar(xln1[:, dt * (TC + 1) + c0:dt * (TC + 1) + c0 + n], sqx[:, :n],
                                       pv8[:, dt * 8 + 0:dt * 8 + 1], pv8[:, dt * 8 + 1:dt * 8 + 2], OP.mult, OP.add))
            ph.on("vector", mkln)
            pbc_v[id(pbc2)] = sv.n
            pbc_v[id(pbc3)] = sv.n
    sv_ln = sv.n

    for dt in range(NCT):
        def mhal(v, dt=dt):
            v.wait_ge(sv.sem, sv_ln)
            sv.inc(v.tensor_scalar(xln1[:, dt * (TC + 1):dt * (TC + 1) + 1],
                                   xln1[:, dt * (TC + 1):dt * (TC + 1) + 1],
                                   pv8[:, dt * 8 + 7:dt * 8 + 8], None, OP.mult))
        ph.on("vector", mhal)
    sv_mask = sv.n
    ph.free("sqb2", "sqx", "xldf", "s_xn1h")

    dmx = ph.sbuf("dmx", [128, TC], BF16)
    for dt in range(NCT):
        def mkmix(v, dt=dt):
            v.wait_ge(sv.sem, sv_mask)
            base = dt * (TC + 1)
            D(v, v.tensor_tensor(dmx[:], xln1[:, base + 1:base + 1 + TC], xln1[:, base:base + TC], OP.subtract))
            v.scalar_tensor_tensor(xkm[:, dt * TC:(dt + 1) * TC], dmx[:], pv8[:, dt * 8 + 2:dt * 8 + 3],
                                   xln1[:, base:base + TC], OP.mult, OP.add)
            v.scalar_tensor_tensor(xvm[:, dt * TC:(dt + 1) * TC], dmx[:], pv8[:, dt * 8 + 3:dt * 8 + 4],
                                   xln1[:, base:base + TC], OP.mult, OP.add)
            sv.inc(v.scalar_tensor_tensor(xrm[:, dt * TC:(dt + 1) * TC], dmx[:], pv8[:, dt * 8 + 4:dt * 8 + 5],
                                          xln1[:, base:base + TC], OP.mult, OP.add))
        ph.on("vector", mkmix)
    sv_mix = sv.n
    ph.free("dmx", "rwb", "rows", "xln1")

    def load_weight_bf(name, dram, wbuf, kdim=1024):
        nt = kdim // 128
        wf = ph.sbuf(name + "_f", [128, 1024], F32)
        snap = (sv.n, sa.n, stt.n, st.n)
        for ctk in range(nt):
            prev_cast = sg.n

            def loadf(e, ctk=ctk, prev_cast=prev_cast, snap=snap):
                if ctk == 0:
                    e.wait_ge(sv.sem, snap[0])
                    e.wait_ge(sa.sem, snap[1])
                    e.wait_ge(stt.sem, snap[2])
                    e.wait_ge(st.sem, snap[3])
                e.wait_ge(sg.sem, prev_cast)
                ld.inc(e.dma_start(out=wf[:], in_=dram[ctk * 128:(ctk + 1) * 128, :]))
            ph.on("sync", loadf)
            ldv = ld.n

            def cast(g, ctk=ctk, ldv=ldv):
                g.wait_ge(ld.sem, ldv)
                sg.inc(g.tensor_copy(wbuf[:, ctk * 1024:(ctk + 1) * 1024], wf[:]))
            ph.on("gpsimd", cast)
        ph.free(name + "_f")
        return sg.n

    wk = ph.sbuf("wk2", [128, NCT * 1024], BF16)
    sg_wk = load_weight_bf("wk2_ld", io["WkT1"], wk)
    wv = ph.sbuf("wv2", [128, NCT * 1024], BF16)
    sg_wv = load_weight_bf("wv2_ld", io["WvT1"], wv)

    kfv = ph.sbuf("kfv", [128, TC], F32)
    vfv = ph.sbuf("vfv", [128, TC], F32)
    t1b = ph.sbuf("t1b", [128, TC], F32)
    t5b = ph.sbuf("t5b", [128, TC], F32)
    t6b = ph.sbuf("t6b", [128, TC], F32)
    ewr = ph.sbuf("ewr", [128, TC], F32)
    scA = ph.sbuf("scA", [128, TC + 1], F32)
    scB = ph.sbuf("scB", [128, TC + 1], F32)
    wcol = ph.sbuf("wcol", [128, 2], F32)
    ends2 = ph.sbuf("ends2", [128, 2], F32)
    pk = [ph.psum(f"p2k{i}", [128, 512], F32) for i in range(2)]
    bank_v = {}
    st_dt = [0]
    sa_dt = [0]
    sv_scan_prev = [0]

    for dt in range(NCT):
        for (src_m, wv_, sg_w, dst) in ((xkm, wk, sg_wk, kfv), (xvm, wv, sg_wv, vfv)):
            for (c0, n) in CH_T:
                bank = pk[(c0 // 512) % 2]

                def mmk(t, dt=dt, c0=c0, n=n, bank=bank, src_m=src_m, wv_=wv_, sg_w=sg_w):
                    t.wait_ge(sg.sem, sg_w)
                    t.wait_ge(sv.sem, max(sv_mix, bank_v.get(id(bank), 0)))
                    i = None
                    for ct in range(NCT):
                        i = t.matmul(bank[:, :n], wv_[:, ct * 1024 + dt * 128:ct * 1024 + (dt + 1) * 128],
                                     src_m[:, ct * TC + c0:ct * TC + c0 + n],
                                     start=(ct == 0), stop=(ct == NCT - 1))
                    stt.inc(i)
                ph.on("tensor", mmk)
                t_mm = stt.n

                def evk(v, c0=c0, n=n, bank=bank, dst=dst, t_mm=t_mm):
                    v.wait_ge(stt.sem, t_mm)
                    v.wait_ge(sg.sem, sg_wv)
                    v.wait_ge(sa.sem, sa_dt[0])
                    v.wait_ge(sv.sem, sv_scan_prev[0])
                    sv.inc(v.tensor_copy(dst[:, c0:c0 + n], bank[:, :n]))
                ph.on("vector", evk)
                bank_v[id(bank)] = sv.n
        sv_kv = sv.n

        if "kdbg" in io:
            def dbgkv(g, dt=dt, sv_kv=sv_kv):
                g.wait_ge(sv.sem, sv_kv)
                st.inc(g.dma_start(out=io["kdbg"][dt * 128:(dt + 1) * 128, :], in_=kfv[:]))
                st.inc(g.dma_start(out=io["vdbg"][dt * 128:(dt + 1) * 128, :], in_=vfv[:]))
            ph.on("gpsimd", dbgkv)
            st_dt[0] = st.n

        def ew1(a, dt=dt, sv_kv=sv_kv, st_dbg=st_dt[0]):
            a.wait_ge(sv.sem, sv_kv)
            a.wait_ge(st.sem, st_dbg)
            a.activation(t1b[:], kfv[:], AF.Exp)
            a.activation(t5b[:], kfv[:], AF.Exp, bias=pv8[:, dt * 8 + 6:dt * 8 + 7])
            sa.inc(a.activation(wcol[:, 0:1], pv8[:, dt * 8 + 5:dt * 8 + 6], AF.Exp))
            a.wait_ge(sa.sem, sa.n)
            sa.inc(a.activation(wcol[:, 1:2], wcol[:, 0:1], AF.Exp, scale=-1.0))
        ph.on("scalar", ew1)
        sa_ew = sa.n
        sa_dt[0] = sa_ew

        def ew2(v, dt=dt, sa_ew=sa_ew, st_prev=st_dt[0]):
            v.wait_ge(sa.sem, sa_ew)
            v.wait_ge(st.sem, st_prev)
            D(v, v.tensor_tensor(t6b[:], t5b[:], vfv[:], OP.mult))
            D(v, v.tensor_tensor(kfv[:], t1b[:], vfv[:], OP.mult))
            D(v, v.tensor_scalar(ewr[:], onestb[:], wcol[:, 1:2], None, OP.mult))
            v.memset(scA[:, 0:1], 0.0)
            D(v, v.memset(scB[:, 0:1], 0.0))
            D(v, v.tensor_tensor_scan(scA[:, 1:TC + 1], ewr[:], kfv[:], 0.0, OP.mult, OP.add))
            D(v, v.tensor_tensor_scan(scB[:, 1:TC + 1], ewr[:], t1b[:], 0.0, OP.mult, OP.add))
            v.tensor_copy(ends2[:, 0:1], scA[:, TC:TC + 1])
            sv.inc(v.tensor_copy(ends2[:, 1:2], scB[:, TC:TC + 1]))
        ph.on("vector", ew2)
        sv_scan = sv.n
        sv_scan_prev[0] = sv_scan

        def stores(g, dt=dt, sv_scan=sv_scan, sv_kv=sv_kv):
            g.wait_ge(sv.sem, sv_scan)
            st.inc(g.dma_start(out=io["scanA"][dt * 128:(dt + 1) * 128, :], in_=scA[:, 0:TC]))
            st.inc(g.dma_start(out=io["scanB"][dt * 128:(dt + 1) * 128, :], in_=scB[:, 0:TC]))
            st.inc(g.dma_start(out=io["t5d"][dt * 128:(dt + 1) * 128, :], in_=t5b[:]))
            st.inc(g.dma_start(out=io["t6d"][dt * 128:(dt + 1) * 128, :], in_=t6b[:]))
            st.inc(g.dma_start(out=io["ends"][dt * 128:(dt + 1) * 128, :], in_=ends2[:]))
        ph.on("gpsimd", stores)
        st_dt[0] = st.n

    ph.free("p2k1", "p2k0")
    ph.free("ends2", "wcol", "scB", "scA", "ewr", "t6b", "t5b", "t1b", "vfv", "kfv")
    ph.free("wv2", "wk2")
    wr = ph.sbuf("wr2", [128, NCT * 1024], BF16)
    sg_wr = load_weight_bf("wr2_ld", io["WrT1"], wr)
    srb = ph.sbuf("srb", [128, TC], BF16)
    pk2 = [ph.psum(f"p2r{i}", [128, 512], F32) for i in range(2)]
    bank_a = {}
    sr_st = [0]
    for dt in range(NCT):
        for (c0, n) in CH_T:
            bank = pk2[(c0 // 512) % 2]

            def mmr(t, dt=dt, c0=c0, n=n, bank=bank):
                t.wait_ge(sg.sem, sg_wr)
                t.wait_ge(sa.sem, bank_a.get(id(bank), 0))
                i = None
                for ct in range(NCT):
                    i = t.matmul(bank[:, :n], wr[:, ct * 1024 + dt * 128:ct * 1024 + (dt + 1) * 128],
                                 xrm[:, ct * TC + c0:ct * TC + c0 + n],
                                 start=(ct == 0), stop=(ct == NCT - 1))
                stt.inc(i)
            ph.on("tensor", mmr)
            t_mr = stt.n

            def ear(a, c0=c0, n=n, bank=bank, t_mr=t_mr):
                a.wait_ge(stt.sem, t_mr)
                if c0 == 0:
                    a.wait_ge(st.sem, sr_st[0])
                sa.inc(a.activation(srb[:, c0:c0 + n], bank[:, :n], AF.Sigmoid))
            ph.on("scalar", ear)
            bank_a[id(bank)] = sa.n
        sa_r = sa.n

        def strr(g, dt=dt, sa_r=sa_r):
            g.wait_ge(sa.sem, sa_r)
            st.inc(g.dma_start(out=io["srd"][dt * 128:(dt + 1) * 128, :], in_=srb[:]))
        ph.on("gpsimd", strr)
        sr_st[0] = st.n

    def fin(g):
        g.wait_ge(st.sem, st.n)
    ph.on("gpsimd", fin)


def build_phase2a():
    nc = bass.Bass("TRN2", target_bir_lowering=False)
    io = {}
    io["xn1h"] = nc.dram_tensor("xn1h", [C, TC + 1], F32, kind="ExternalInput").ap()
    for nm in ["WkT1", "WvT1", "WrT1"]:
        io[nm] = nc.dram_tensor(nm, [C, C], F32, kind="ExternalInput").ap()
    for nm in ["ln1_g", "ln1_b", "amk", "amv", "amr", "td", "tf", "hm"]:
        io[nm] = nc.dram_tensor(nm, [C, 1], F32, kind="ExternalInput").ap()
    io["scanA"] = nc.dram_tensor("scanA", [C, TC], F32, kind="ExternalOutput").ap()
    io["scanB"] = nc.dram_tensor("scanB", [C, TC], F32, kind="ExternalOutput").ap()
    io["t5d"] = nc.dram_tensor("t5d", [C, TC], F32, kind="ExternalOutput").ap()
    io["t6d"] = nc.dram_tensor("t6d", [C, TC], F32, kind="ExternalOutput").ap()
    io["srd"] = nc.dram_tensor("srd", [C, TC], BF16, kind="ExternalOutput").ap()
    io["ends"] = nc.dram_tensor("ends", [C, 2], F32, kind="ExternalOutput").ap()
    import os
    if os.environ.get("KDBG"):
        io["kdbg"] = nc.dram_tensor("kdbg", [C, TC], F32, kind="ExternalOutput").ap()
        io["vdbg"] = nc.dram_tensor("vdbg", [C, TC], F32, kind="ExternalOutput").ap()
    with ExitStack() as stack:
        ph = Ph(nc, stack)
        emit_phase2a(ph, io)
    return nc



def emit_phase2b(ph, io):
    ld = ph.ctr("bld", 16)
    st = ph.ctr("bst", 16)
    sv = ph.ctr("bsv")
    sa = ph.ctr("bsa")
    sg = ph.ctr("bsg")

    def D(v, i):
        sv.inc(i)
        v.wait_ge(sv.sem, sv.n)

    iot = ph.sbuf("iot", [128, TC], F32)
    delt = ph.sbuf("delt", [128, 4], F32)
    pvtd = ph.sbuf("pvtd", [128, 8], F32)
    coef = ph.sbuf("coef", [128, 4], F32)
    wcol = ph.sbuf("wcol", [128, 6], F32)
    endsb = ph.sbuf("endsb", [128, 8], F32)
    sA = [ph.sbuf(f"sA{i}", [128, TC], F32) for i in range(2)]
    sB = [ph.sbuf(f"sB{i}", [128, TC], F32) for i in range(2)]
    s5 = [ph.sbuf(f"s5{i}", [128, TC], F32) for i in range(2)]
    s6 = [ph.sbuf(f"s6{i}", [128, TC], F32) for i in range(2)]
    ssr = [ph.sbuf(f"ssr{i}", [128, TC], BF16) for i in range(2)]
    dar = ph.sbuf("dar", [128, TC], F32)
    rkb = [ph.sbuf(f"rkb{i}", [128, TC], BF16) for i in range(2)]

    def loadc(e):
        ld.inc(e.dma_start(out=iot[:], in_=io["iota"]))
        ld.inc(e.dma_start(out=delt[:], in_=io["delta"]))
        for ct in range(NCT):
            ld.inc(e.dma_start(out=pvtd[:, ct:ct + 1], in_=io["td"][ct * 128:(ct + 1) * 128, :]))
    ph.on("sync", loadc)
    ld_c = ld.n

    buf_done = {0: 0, 1: 0}
    rk_done = {0: 0, 1: 0}
    for dt in range(NCT):
        par = dt % 2

        def ldd(e, dt=dt, par=par):
            e.wait_ge(sv.sem, buf_done[par])
            ld.inc(e.dma_start(out=sA[par][:], in_=io["scanA"][dt * 128:(dt + 1) * 128, :]))
            ld.inc(e.dma_start(out=sB[par][:], in_=io["scanB"][dt * 128:(dt + 1) * 128, :]))
            ld.inc(e.dma_start(out=s5[par][:], in_=io["t5d"][dt * 128:(dt + 1) * 128, :]))
            ld.inc(e.dma_start(out=s6[par][:], in_=io["t6d"][dt * 128:(dt + 1) * 128, :]))
            ld.inc(e.dma_start(out=ssr[par][:], in_=io["srd"][dt * 128:(dt + 1) * 128, :]))
            ld.inc(e.dma_start(out=endsb[:, :], in_=io["ends_all"][dt * 128:(dt + 1) * 128, :]))
        ph.on("sync", ldd)
        ld_d = ld.n

        def wc(a, dt=dt):
            a.wait_ge(ld.sem, ld_c)
            sa.inc(a.activation(wcol[:, 0:1], pvtd[:, dt:dt + 1], AF.Exp))
            a.wait_ge(sa.sem, sa.n)
            sa.inc(a.activation(wcol[:, 1:2], wcol[:, 0:1], AF.Exp, scale=-1.0))
        ph.on("scalar", wc)
        sa_w = sa.n

        def wneg(v, sa_w=sa_w):
            v.wait_ge(sa.sem, sa_w)
            D(v, v.tensor_scalar(wcol[:, 2:3], wcol[:, 0:1], -1.0, None, OP.mult))
            D(v, v.tensor_scalar(coef[:], delt[:], wcol[:, 2:3], None, OP.mult))
            sv.inc(v.tensor_scalar(dar[:], iot[:], wcol[:, 2:3], None, OP.mult))
        ph.on("vector", wneg)
        sv_wn = sv.n

        def ce(a, sv_wn=sv_wn):
            a.wait_ge(sv.sem, sv_wn)
            sa.inc(a.activation(coef[:], coef[:], AF.Exp))
            a.wait_ge(sa.sem, sa.n)
            sa.inc(a.activation(dar[:], dar[:], AF.Exp))
        ph.on("scalar", ce)
        sa_ce = sa.n

        def comb(v, dt=dt, par=par, sa_ce=sa_ce, ld_d=ld_d):
            v.wait_ge(sa.sem, sa_ce)
            v.wait_ge(ld.sem, ld_d)
            v.wait_ge(st.sem, rk_done[par])
            D(v, v.tensor_tensor(wcol[:, 3:4], coef[:, 0:1], endsb[:, 0:1], OP.mult))
            D(v, v.tensor_tensor(wcol[:, 4:5], coef[:, 1:2], endsb[:, 1:2], OP.mult))
            D(v, v.tensor_tensor(wcol[:, 3:4], wcol[:, 3:4], wcol[:, 4:5], OP.add))
            D(v, v.tensor_tensor(wcol[:, 4:5], coef[:, 2:3], endsb[:, 2:3], OP.mult))
            D(v, v.tensor_tensor(wcol[:, 3:4], wcol[:, 3:4], wcol[:, 4:5], OP.add))
            D(v, v.tensor_tensor(wcol[:, 4:5], coef[:, 3:4], endsb[:, 3:4], OP.mult))
            D(v, v.tensor_tensor(wcol[:, 3:4], wcol[:, 3:4], wcol[:, 4:5], OP.add))
            D(v, v.tensor_tensor(wcol[:, 4:5], coef[:, 0:1], endsb[:, 4:5], OP.mult))
            D(v, v.tensor_tensor(wcol[:, 5:6], coef[:, 1:2], endsb[:, 5:6], OP.mult))
            D(v, v.tensor_tensor(wcol[:, 4:5], wcol[:, 4:5], wcol[:, 5:6], OP.add))
            D(v, v.tensor_tensor(wcol[:, 5:6], coef[:, 2:3], endsb[:, 6:7], OP.mult))
            D(v, v.tensor_tensor(wcol[:, 4:5], wcol[:, 4:5], wcol[:, 5:6], OP.add))
            D(v, v.tensor_tensor(wcol[:, 5:6], coef[:, 3:4], endsb[:, 7:8], OP.mult))
            D(v, v.tensor_tensor(wcol[:, 4:5], wcol[:, 4:5], wcol[:, 5:6], OP.add))
            D(v, v.scalar_tensor_tensor(sA[par][:], dar[:], wcol[:, 3:4], sA[par][:], OP.mult, OP.add))
            D(v, v.tensor_tensor(sA[par][:], sA[par][:], s6[par][:], OP.add))
            D(v, v.scalar_tensor_tensor(sB[par][:], dar[:], wcol[:, 4:5], sB[par][:], OP.mult, OP.add))
            D(v, v.tensor_tensor(sB[par][:], sB[par][:], s5[par][:], OP.add))
            D(v, v.reciprocal(sB[par][:], sB[par][:]))
            D(v, v.tensor_tensor(sA[par][:], sA[par][:], sB[par][:], OP.mult))
            sv.inc(v.tensor_tensor(rkb[par][:], ssr[par][:], sA[par][:], OP.mult))
            buf_done[par] = sv.n
        ph.on("vector", comb)
        sv_y = sv.n

        def str_(g, dt=dt, par=par, sv_y=sv_y):
            g.wait_ge(sv.sem, sv_y)
            st.inc(g.dma_start(out=io["rwkv"][dt * 128:(dt + 1) * 128, :], in_=rkb[par][:]))
        ph.on("gpsimd", str_)
        rk_done[par] = st.n

    def fin(g):
        g.wait_ge(st.sem, st.n)
    ph.on("gpsimd", fin)


def build_phase2b():
    nc = bass.Bass("TRN2", target_bir_lowering=False)
    io = {}
    for nm in ["scanA", "scanB", "t5d", "t6d"]:
        io[nm] = nc.dram_tensor(nm, [C, TC], F32, kind="ExternalInput").ap()
    io["srd"] = nc.dram_tensor("srd", [C, TC], BF16, kind="ExternalInput").ap()
    io["ends_all"] = nc.dram_tensor("ends_all", [C, 8], F32, kind="ExternalInput").ap()
    io["iota"] = nc.dram_tensor("iota", [128, TC], F32, kind="ExternalInput").ap()
    io["delta"] = nc.dram_tensor("delta", [128, 4], F32, kind="ExternalInput").ap()
    io["td"] = nc.dram_tensor("td", [C, 1], F32, kind="ExternalInput").ap()
    io["rwkv"] = nc.dram_tensor("rwkv", [C, TC], BF16, kind="ExternalOutput").ap()
    with ExitStack() as stack:
        ph = Ph(nc, stack)
        emit_phase2b(ph, io)
    return nc


def _vec_col(x):
    return np.ascontiguousarray(np.asarray(x).reshape(-1, 1).astype(np.float32))


def _prep_phase2a(inputs, x1s, xn1s):
    common = dict(
        WkT1=_hostT(inputs["L1_attn_Wk"]), WvT1=_hostT(inputs["L1_attn_Wv"]),
        WrT1=_hostT(inputs["L1_attn_Wr"]),
        ln1_g=_vec_col(inputs["L1_attn_ln1_g"]), ln1_b=_vec_col(inputs["L1_attn_ln1_b"]),
        amk=_vec_col(inputs["L1_attn_att_mix_k"]), amv=_vec_col(inputs["L1_attn_att_mix_v"]),
        amr=_vec_col(inputs["L1_attn_att_mix_r"]),
        td=_vec_col(inputs["L1_attn_time_decay"]), tf=_vec_col(inputs["L1_attn_time_first"]),
    )
    per_core = []
    for i in range(8):
        b, j, t0 = _core_meta(i)
        xh = np.zeros((C, TC + 1), np.float32)
        xh[:, 1:] = xn1s[i]
        if j > 0:
            xh[:, 0] = xn1s[i - 1][:, -1]
        hm = np.zeros((C, 1), np.float32) if j == 0 else np.ones((C, 1), np.float32)
        per_core.append(dict(xn1h=np.ascontiguousarray(xh), hm=hm))
    return common, per_core


def _prep_phase2b(inputs, a_res):
    common = dict(
        iota=np.ascontiguousarray(np.tile(np.arange(TC, dtype=np.float32), (128, 1))),
        td=_vec_col(inputs["L1_attn_time_decay"]),
    )
    per_core = []
    for i in range(8):
        b, j, t0 = _core_meta(i)
        ends_all = np.zeros((C, 8), np.float32)
        for r in range(4):
            ends_all[:, r] = a_res[4 * b + r]["ends"][:, 0]
            ends_all[:, 4 + r] = a_res[4 * b + r]["ends"][:, 1]
        delta = np.full((128, 4), 1e9, np.float32)
        for r in range(j):
            delta[:, r] = TC * (j - 1 - r)
        per_core.append(dict(
            scanA=a_res[i]["scanA"], scanB=a_res[i]["scanB"], t5d=a_res[i]["t5d"],
            t6d=a_res[i]["t6d"], srd=a_res[i]["srd"],
            ends_all=np.ascontiguousarray(ends_all), delta=np.ascontiguousarray(delta),
        ))
    return common, per_core


def run_phase2(inputs, x1s, xn1s):
    common, per_core = _prep_phase2a(inputs, x1s, xn1s)
    nc = build_phase2a()
    a_res = run_bass_kernel_spmd(nc, [dict(common, **pc) for pc in per_core], core_ids=list(range(8))).results
    common, per_core = _prep_phase2b(inputs, a_res)
    nc = build_phase2b()
    b_res = run_bass_kernel_spmd(nc, [dict(common, **pc) for pc in per_core], core_ids=list(range(8))).results
    return a_res, b_res



def emit_phase3(ph, io):
    ld = ph.ctr("3ld", 16)
    st = ph.ctr("3st", 16)
    sv = ph.ctr("3sv")
    sa = ph.ctr("3sa")
    stt = ph.ctr("3stt")
    sg = ph.ctr("3sg")

    def D(v, i):
        sv.inc(i)
        v.wait_ge(sv.sem, sv.n)

    consts = ph.sbuf("c3", [128, 4], F32)
    onesb1 = ph.sbuf("onesb1", [128, 1], BF16)
    onesrow = ph.sbuf("onesrow", [1, 128], BF16)
    idb3 = ph.sbuf("idb3", [128, 128], BF16)
    idf3 = ph.sbuf("idf3", [128, 128], F32)
    pv3 = ph.sbuf("pv3", [128, 8 * 6], F32)
    bgc = ph.sbuf("bgc", [128, 1], F32)

    def initc(v):
        v.memset(consts[:, 0:1], 1e-5)
        v.memset(onesb1[:], 1.0)
        sv.inc(v.memset(onesrow[:], 1.0))
    ph.on("vector", initc)
    c_ln = consts[:, 0:1]

    def loadc(e):
        ld.inc(e.dma_start(out=idb3[:], in_=io["idb"]))
        ld.inc(e.dma_start(out=idf3[:], in_=io["idf"]))
        ld.inc(e.dma_start(out=bgc[0:6, :], in_=io["bg"]))
        names = ["ln2_g", "ln2_b", "fmk", "fmr", "hm"]
        for ci, nm in enumerate(names):
            for ct in range(NCT):
                ld.inc(e.dma_start(out=pv3[:, ct * 6 + ci:ct * 6 + ci + 1],
                                   in_=io[nm][ct * 128:(ct + 1) * 128, :]))
    ph.on("sync", loadc)
    ld_c = ld.n

    def load_weight_bf(name, dram, wbuf, kdim, mdim=1024):
        nt = kdim // 128
        wf = ph.sbuf(name + "_f", [128, mdim], F32)
        snap = (sv.n, sa.n, stt.n, st.n)
        for ctk in range(nt):
            prev_cast = sg.n

            def loadf(e, ctk=ctk, prev_cast=prev_cast, snap=snap):
                if ctk == 0:
                    e.wait_ge(sv.sem, snap[0])
                    e.wait_ge(sa.sem, snap[1])
                    e.wait_ge(stt.sem, snap[2])
                    e.wait_ge(st.sem, snap[3])
                e.wait_ge(sg.sem, prev_cast)
                ld.inc(e.dma_start(out=wf[:], in_=dram[ctk * 128:(ctk + 1) * 128, :]))
            ph.on("sync", loadf)
            ldv = ld.n

            def cast(g, ctk=ctk, ldv=ldv):
                g.wait_ge(ld.sem, ldv)
                sg.inc(g.tensor_copy(wbuf[:, ctk * mdim:(ctk + 1) * mdim], wf[:]))
            ph.on("gpsimd", cast)
        ph.free(name + "_f")
        return sg.n

    onesf32v = ph.sbuf("onesf32v", [128, 1], F32)
    def init2(v):
        sv.inc(v.memset(onesf32v[:], 1.0))
    ph.on("vector", init2)
    xk2 = ph.sbuf("xk2", [128, NCT * TC], BF16)
    sr2 = ph.sbuf("sr2", [128, NCT * TC], BF16)
    xr2 = ph.sbuf("xr2", [128, NCT * TC], BF16)
    xn2 = ph.sbuf("xn2", [128, NCT * (TC + 1)], BF16)

    wo1 = ph.sbuf("wo1", [128, NCT * 1024], BF16)
    sg_wo1 = load_weight_bf("wo1_ld", io["WoT1"], wo1, 1024)

    rows = ph.sbuf("rows3", [1, 6 * 512], F32)
    rwbb = ph.sbuf("rwbb", [1, 512], BF16)
    rkc = [ph.sbuf(f"rkc{i}", [128, NCT * 512], BF16) for i in range(2)]
    xnb = [ph.sbuf(f"xnb3{i}", [128, 512], F32) for i in range(2)]
    xmb = [ph.sbuf(f"xmb3{i}", [128, NCT * 512], F32) for i in range(1)]
    sqb = ph.sbuf("sqb3", [128, 512], BF16)
    sqf3 = ph.sbuf("sqf3", [128, 512], F32)
    pmm = [ph.psum(f"p3m{i}", [128, 512], F32) for i in range(2)]
    pnr = ph.psum("p3nr", [1, 512], F32)
    pnr2 = ph.psum("p3nr2", [1, 512], F32)
    pbc2 = ph.psum("p3bc2", [128, 512], F32)
    pbc3 = ph.psum("p3bc3", [128, 512], F32)
    bank_v = {}
    bank_a = {}
    pnr_v = [0]
    pnr_a = [0]
    pnr2_v = [0]
    sq_g = [0]
    xn_free = {0: 0, 1: 0}

    rk_free = {0: 0, 1: 0}
    for ci, (c0, n) in enumerate(CH_X):
        par = ci % 2
        xm = xmb[0]
        rk = rkc[par]

        def ldrk(e, c0=c0, n=n, rk=rk, par=par, prev=rk_free[ci % 2]):
            e.wait_ge(stt.sem, prev)
            for ct in range(NCT):
                ld.inc(e.dma_start(out=rk[:, ct * 512:ct * 512 + n],
                                   in_=io["rwkvh"][ct * 128:(ct + 1) * 128, c0:c0 + n],
                                   allow_slow_non_contiguous=(n == 1)))
        ph.on("sync", ldrk)
        ld_rk = ld.n

        for dt in range(NCT):
            bank = pmm[dt % 2]

            def mmwo(t, dt=dt, c0=c0, n=n, bank=bank, rk=rk, ld_rk=ld_rk):
                t.wait_ge(sg.sem, sg_wo1)
                t.wait_ge(ld.sem, ld_rk)
                t.wait_ge(sv.sem, bank_v.get(id(bank), 0))
                i = None
                for ct in range(NCT):
                    i = t.matmul(bank[:, :n], wo1[:, ct * 1024 + dt * 128:ct * 1024 + (dt + 1) * 128],
                                 rk[:, ct * 512:ct * 512 + n],
                                 start=(ct == 0), stop=(ct == NCT - 1))
                stt.inc(i)
            ph.on("tensor", mmwo)
            t_wo = stt.n
            if dt == NCT - 1:
                rk_free[ci % 2] = stt.n

            def ldxn(e, dt=dt, c0=c0, n=n):
                e.wait_ge(sv.sem, xn_free[dt % 2])
                ld.inc(e.dma_start(out=xnb[dt % 2][:, :n], in_=io["xn1h"][dt * 128:(dt + 1) * 128, c0:c0 + n],
                                   allow_slow_non_contiguous=(n == 1)))
            ph.on("sync", ldxn)
            ld_xn = ld.n

            def evxm(v, dt=dt, n=n, bank=bank, xm=xm, t_wo=t_wo, ld_xn=ld_xn, ci=ci,
                     barrier=(None if ci < 1 else (sa.n, st.n)) if False else None,
                     ba=sa.n, bs=st.n):
                v.wait_ge(stt.sem, t_wo)
                v.wait_ge(ld.sem, ld_xn)
                if dt == 0 and ci >= 1:
                    v.wait_ge(sa.sem, ba)
                    v.wait_ge(st.sem, bs)
                sv.inc(v.tensor_tensor(xm[:, dt * 512:dt * 512 + n], xnb[dt % 2][:, :n], bank[:, :n], OP.add))
                xn_free[dt % 2] = sv.n
            ph.on("vector", evxm)
            sv_xm = sv.n
            bank_v[id(bank)] = sv_xm

            def stxm(g, dt=dt, c0=c0, n=n, xm=xm, sv_xm=sv_xm):
                g.wait_ge(sv.sem, sv_xm)
                st.inc(g.dma_start(out=io["xmid"][dt * 128:(dt + 1) * 128, c0:c0 + n], in_=xm[:, dt * 512:dt * 512 + n],
                                   allow_slow_non_contiguous=(n == 1)))
            ph.on("gpsimd", stxm)

            def smm1(t, dt=dt, n=n, xm=xm, sv_xm=sv_xm):
                t.wait_ge(sv.sem, sv_xm)
                if dt == 0:
                    t.wait_ge(sv.sem, pnr_v[0])
                    t.wait_ge(sa.sem, pnr_a[0])
                stt.inc(t.matmul(pnr[0:1, :n], onesf32v[:, 0:1],
                                 xm[:, dt * 512:dt * 512 + n], start=(dt == 0), stop=(dt == NCT - 1)))
            ph.on("tensor", smm1)

            def sqx1(a, dt=dt, n=n, xm=xm, sv_xm=sv_xm):
                a.wait_ge(sv.sem, sv_xm)
                a.wait_ge(stt.sem, sq_g[0])
                sa.inc(a.activation(sqb[:, :n], xm[:, dt * 512:dt * 512 + n], AF.Square))
            ph.on("scalar", sqx1)
            sa_s = sa.n

            def smm2(t, dt=dt, n=n, sa_s=sa_s):
                t.wait_ge(sa.sem, sa_s)
                if dt == 0:
                    t.wait_ge(sv.sem, pnr2_v[0])
                stt.inc(t.matmul(pnr2[0:1, :n], onesb1[:, 0:1], sqb[:, :n],
                                 start=(dt == 0), stop=(dt == NCT - 1)))
            ph.on("tensor", smm2)
            sq_g[0] = stt.n
        t_st = stt.n

        def lnr1(v, n=n, t_st=t_st):
            v.wait_ge(stt.sem, t_st)
            D(v, v.tensor_scalar(rows[0:1, 0:n], pnr[0:1, :n], 1.0 / C, None, OP.mult))
            D(v, v.tensor_scalar(rows[0:1, 512:512 + n], pnr2[0:1, :n], 1.0 / C, None, OP.mult))
            D(v, v.tensor_tensor(rows[0:1, 1024:1024 + n], rows[0:1, 0:n], rows[0:1, 0:n], OP.mult))
            sv.inc(v.tensor_tensor(rows[0:1, 1024:1024 + n], rows[0:1, 512:512 + n], rows[0:1, 1024:1024 + n], OP.subtract))
        ph.on("vector", lnr1)
        sv_var = sv.n
        pnr_v[0] = sv_var
        pnr2_v[0] = sv_var

        def lnr2(a, n=n, sv_var=sv_var):
            a.wait_ge(sv.sem, sv_var)
            sa.inc(a.activation(rows[0:1, 1536:1536 + n], rows[0:1, 1024:1024 + n], AF.Sqrt, bias=c_ln[0:1]))
        ph.on("scalar", lnr2)
        sa_sd = sa.n
        pnr_a[0] = sa_sd

        def lnr3(v, n=n, sa_sd=sa_sd):
            v.wait_ge(sa.sem, sa_sd)
            D(v, v.reciprocal(rows[0:1, 2048:2048 + n], rows[0:1, 1536:1536 + n]))
            D(v, v.tensor_tensor(rows[0:1, 2560:2560 + n], rows[0:1, 0:n], rows[0:1, 2048:2048 + n], OP.mult))
            sv.inc(v.tensor_copy(rwbb[0:1, :n], rows[0:1, 2048:2048 + n]))
        ph.on("vector", lnr3)
        sv_r = sv.n

        def bcr(t, n=n, sv_r=sv_r):
            t.wait_ge(sv.sem, max(sv_r, bank_v.get(id(pbc2), 0)))
            stt.inc(t.matmul(pbc2[:, :n], onesrow[0:1, :], rwbb[0:1, :n], start=True, stop=True))
        ph.on("tensor", bcr)
        t_b2 = stt.n

        def mrr(v, n=n, t_b2=t_b2):
            v.wait_ge(stt.sem, t_b2)
            sv.inc(v.tensor_copy(rwbb[0:1, :n], rows[0:1, 2560:2560 + n]))
        ph.on("vector", mrr)
        sv_mr = sv.n

        def bcm(t, n=n, sv_mr=sv_mr):
            t.wait_ge(sv.sem, max(sv_mr, bank_v.get(id(pbc3), 0)))
            stt.inc(t.matmul(pbc3[:, :n], onesrow[0:1, :], rwbb[0:1, :n], start=True, stop=True))
        ph.on("tensor", bcm)
        t_b3 = stt.n

        for dt in range(NCT):
            def mkxn2(v, dt=dt, c0=c0, n=n, xm=xm, t_b3=t_b3):
                v.wait_ge(stt.sem, t_b3)
                v.wait_ge(ld.sem, ld_c)
                D(v, v.tensor_tensor(sqf3[:, :n], xm[:, dt * 512:dt * 512 + n], pbc2[:, :n], OP.mult))
                D(v, v.tensor_tensor(sqf3[:, :n], sqf3[:, :n], pbc3[:, :n], OP.subtract))
                sv.inc(v.tensor_scalar(xn2[:, dt * (TC + 1) + c0:dt * (TC + 1) + c0 + n], sqf3[:, :n],
                                       pv3[:, dt * 6 + 0:dt * 6 + 1], pv3[:, dt * 6 + 1:dt * 6 + 2], OP.mult, OP.add))
            ph.on("vector", mkxn2)
            bank_v[id(pbc2)] = sv.n
            bank_v[id(pbc3)] = sv.n
    sv_xn2 = sv.n

    for dt in range(NCT):
        def mhal(v, dt=dt):
            v.wait_ge(sv.sem, sv_xn2)
            sv.inc(v.tensor_scalar(xn2[:, dt * (TC + 1):dt * (TC + 1) + 1],
                                   xn2[:, dt * (TC + 1):dt * (TC + 1) + 1],
                                   pv3[:, dt * 6 + 4:dt * 6 + 5], None, OP.mult))
        ph.on("vector", mhal)
    sv_mask = sv.n

    dmx3 = ph.sbuf("dmx3", [128, TC], BF16)
    for dt in range(NCT):
        def mkmix(v, dt=dt):
            v.wait_ge(sv.sem, sv_mask)
            base = dt * (TC + 1)
            D(v, v.tensor_tensor(dmx3[:], xn2[:, base + 1:base + 1 + TC], xn2[:, base:base + TC], OP.subtract))
            v.scalar_tensor_tensor(xk2[:, dt * TC:(dt + 1) * TC], dmx3[:], pv3[:, dt * 6 + 2:dt * 6 + 3],
                                   xn2[:, base:base + TC], OP.mult, OP.add)
            sv.inc(v.scalar_tensor_tensor(xr2[:, dt * TC:(dt + 1) * TC], dmx3[:], pv3[:, dt * 6 + 3:dt * 6 + 4],
                                          xn2[:, base:base + TC], OP.mult, OP.add))
        ph.on("vector", mkmix)
    sv_mix = sv.n
    ph.free("dmx3")
    ph.free("sqf3", "sqb3", "xmb30", "xnb31", "xnb30", "rkc1", "rkc0", "rwbb", "rows3", "wo1", "xn2")
    ph.free("p3bc3", "p3bc2", "p3nr2", "p3nr")

    wrf = ph.sbuf("wrf", [128, NCT * 1024], BF16)
    sg_wrf = load_weight_bf("wrf_ld", io["WrfT"], wrf, 1024)
    for dt in range(NCT):
        for (c0, n) in CH_T:
            bank = pmm[(c0 // 512) % 2]

            def mmsr(t, dt=dt, c0=c0, n=n, bank=bank):
                t.wait_ge(sg.sem, sg_wrf)
                t.wait_ge(sv.sem, max(sv_mix, bank_v.get(id(bank), 0)))
                t.wait_ge(sa.sem, bank_a.get(id(bank), 0))
                i = None
                for ct in range(NCT):
                    i = t.matmul(bank[:, :n], wrf[:, ct * 1024 + dt * 128:ct * 1024 + (dt + 1) * 128],
                                 xr2[:, ct * TC + c0:ct * TC + c0 + n],
                                 start=(ct == 0), stop=(ct == NCT - 1))
                stt.inc(i)
            ph.on("tensor", mmsr)
            t_ms = stt.n

            def easr(a, dt=dt, c0=c0, n=n, bank=bank, t_ms=t_ms):
                a.wait_ge(stt.sem, t_ms)
                a.wait_ge(sg.sem, sg_wrf)
                sa.inc(a.activation(sr2[:, dt * TC + c0:dt * TC + c0 + n], bank[:, :n], AF.Sigmoid))
            ph.on("scalar", easr)
            bank_a[id(bank)] = sa.n
    sa_sr2 = sa.n
    ph.free("wrf")

    kfb = ph.sbuf("kfb", [128, 16 * 512], BF16)
    krt3 = ph.sbuf("krt3", [128, 512], BF16)
    kf_t = [0]
    x1c = [ph.sbuf(f"x1c{i}", [128, 512], F32) for i in range(2)]
    kvc = [ph.sbuf(f"kvc{i}", [128, 512], F32) for i in range(2)]
    xmc3 = [ph.sbuf(f"xmc3{i}", [128, 512], F32) for i in range(2)]
    o3 = [ph.sbuf(f"o3{i}", [128, 512], F32) for i in range(2)]
    kf_a = [0]
    kv_st = {0: 0, 1: 0}
    kv_sv = {0: 0, 1: 0}
    o3_free = {0: 0, 1: 0}

    for fh in range(2):
        wkf = ph.sbuf(f"wkf{fh}", [128, NCT * 2048], BF16)
        sg_wkf = load_weight_bf(f"wkf_ld{fh}", io["WkfT"][:, fh * 2048:(fh + 1) * 2048], wkf, 1024, mdim=2048)
        wvf = ph.sbuf(f"wvf{fh}", [128, 16 * 1024], BF16)
        sg_wvf = load_weight_bf(f"wvf_ld{fh}", io["WvfT"][fh * 2048:(fh + 1) * 2048, :], wvf, 2048)

        for ci, (c0, n) in enumerate(CH_T):
            for ft in range(16):
                bank = pmm[ft % 2]

                def mmkf(t, ft=ft, c0=c0, n=n, bank=bank, sg_wkf=sg_wkf, sg_wvf=sg_wvf, fh=fh):
                    t.wait_ge(sg.sem, max(sg_wkf, sg_wvf))
                    t.wait_ge(sa.sem, max(sa_sr2, bank_a.get(id(bank), 0), kf_a[0] if ft < 2 else 0))
                    t.wait_ge(sv.sem, bank_v.get(id(bank), 0))
                    i = None
                    for ct in range(NCT):
                        i = t.matmul(bank[:, :n], wkf[:, ct * 2048 + ft * 128:ct * 2048 + (ft + 1) * 128],
                                     xk2[:, ct * TC + c0:ct * TC + c0 + n],
                                     start=(ct == 0), stop=(ct == NCT - 1))
                    stt.inc(i)
                ph.on("tensor", mmkf)
                t_kf = stt.n

                def evkf(v, ft=ft, n=n, bank=bank, t_kf=t_kf, prev_sq=kf_a[0]):
                    v.wait_ge(stt.sem, t_kf)
                    v.wait_ge(sa.sem, prev_sq)
                    sv.inc(v.tensor_scalar(krt3[:, :n], bank[:, :n], 0.0, None, OP.max))
                ph.on("vector", evkf)
                sv_kr = sv.n
                bank_v[id(bank)] = sv_kr

                def eakf(a, ft=ft, n=n, sv_kr=sv_kr, prev_kv=kf_t[0]):
                    a.wait_ge(sv.sem, sv_kr)
                    a.wait_ge(stt.sem, prev_kv)
                    sa.inc(a.activation(kfb[:, ft * 512:ft * 512 + n], krt3[:, :n], AF.Square))
                ph.on("scalar", eakf)
                kf_a[0] = sa.n
            sa_kf = sa.n

            for ct_out in range(NCT):
                bank = pmm[ct_out % 2]

                def mmkv(t, ct_out=ct_out, c0=c0, n=n, bank=bank, sa_kf=sa_kf):
                    t.wait_ge(sa.sem, sa_kf)
                    t.wait_ge(sv.sem, bank_v.get(id(bank), 0))
                    i = None
                    for ft in range(16):
                        i = t.matmul(bank[:, :n], wvf[:, ft * 1024 + ct_out * 128:ft * 1024 + (ct_out + 1) * 128],
                                     kfb[:, ft * 512:ft * 512 + n],
                                     start=(ft == 0), stop=(ft == 15))
                    stt.inc(i)
                ph.on("tensor", mmkv)
                t_kv = stt.n

                if fh == 0:
                    def evkv(v, ct_out=ct_out, n=n, bank=bank, t_kv=t_kv):
                        v.wait_ge(stt.sem, t_kv)
                        v.wait_ge(st.sem, kv_st[ct_out % 2])
                        sv.inc(v.tensor_copy(kvc[ct_out % 2][:, :n], bank[:, :n]))
                    ph.on("vector", evkv)
                    sv_kv = sv.n
                    bank_v[id(bank)] = sv_kv

                    def stkv(g, ct_out=ct_out, c0=c0, n=n, sv_kv=sv_kv):
                        g.wait_ge(sv.sem, sv_kv)
                        st.inc(g.dma_start(out=io["kvd"][ct_out * 128:(ct_out + 1) * 128, c0:c0 + n],
                                           in_=kvc[ct_out % 2][:, :n]))
                    ph.on("gpsimd", stkv)
                    kv_st[ct_out % 2] = st.n
                else:
                    def ldprev(e, ct_out=ct_out, c0=c0, n=n, st_w=kv_st[ct_out % 2]):
                        e.wait_ge(st.sem, st_w)
                        e.wait_ge(sv.sem, kv_sv[ct_out % 2])
                        ld.inc(e.dma_start(out=kvc[ct_out % 2][:, :n], in_=io["kvd"][ct_out * 128:(ct_out + 1) * 128, c0:c0 + n]))
                        ld.inc(e.dma_start(out=x1c[ct_out % 2][:, :n], in_=io["x1"][ct_out * 128:(ct_out + 1) * 128, c0:c0 + n]))
                        ld.inc(e.dma_start(out=xmc3[ct_out % 2][:, :n], in_=io["xmid"][ct_out * 128:(ct_out + 1) * 128, 1 + c0:1 + c0 + n]))
                    ph.on("sync", ldprev)
                    ld_pv = ld.n

                    def evfin(v, ct_out=ct_out, c0=c0, n=n, bank=bank, t_kv=t_kv, ld_pv=ld_pv):
                        v.wait_ge(stt.sem, t_kv)
                        v.wait_ge(ld.sem, ld_pv)
                        v.wait_ge(st.sem, o3_free[ct_out % 2])
                        D(v, v.tensor_tensor(kvc[ct_out % 2][:, :n], kvc[ct_out % 2][:, :n], bank[:, :n], OP.add))
                        D(v, v.tensor_tensor(o3[ct_out % 2][:, :n], sr2[:, ct_out * TC + c0:ct_out * TC + c0 + n],
                                             kvc[ct_out % 2][:, :n], OP.mult))
                        D(v, v.tensor_tensor(o3[ct_out % 2][:, :n], o3[ct_out % 2][:, :n], xmc3[ct_out % 2][:, :n], OP.add))
                        sv.inc(v.tensor_tensor(o3[ct_out % 2][:, :n], o3[ct_out % 2][:, :n], x1c[ct_out % 2][:, :n], OP.add))
                        kv_sv[ct_out % 2] = sv.n
                    ph.on("vector", evfin)
                    sv_fin = sv.n
                    bank_v[id(bank)] = sv_fin

                    def sto3(g, ct_out=ct_out, c0=c0, n=n, sv_fin=sv_fin):
                        g.wait_ge(sv.sem, sv_fin)
                        st.inc(g.dma_start(out=io["out"][ct_out * 128:(ct_out + 1) * 128, c0:c0 + n],
                                           in_=o3[ct_out % 2][:, :n]))
                    ph.on("gpsimd", sto3)
                    o3_free[ct_out % 2] = st.n
            kf_t[0] = stt.n
        ph.free(f"wvf{fh}", f"wkf{fh}")
    sv_ffn_done = sv.n
    st_ffn = st.n
    ph.free("o31", "o30", "xmc31", "xmc30", "kvc1", "kvc0", "x1c1", "x1c0", "kf_t_dummy" if False else "krt3")
    ph.free("kfb")

    ph.free("xr2", "sr2", "xk2")
    xn1b = ph.sbuf("xn1b", [128, NCT * TC], BF16)
    xldf3 = ph.sbuf("xldf3", [128, TC + 1], F32)
    snapm = (sv.n, sa.n, stt.n, st.n)
    cast_done = 0
    for ct in range(NCT):
        prev = cast_done

        def ldt(e, ct=ct, prev=prev, snapm=snapm):
            if ct == 0:
                e.wait_ge(sv.sem, snapm[0])
                e.wait_ge(sa.sem, snapm[1])
                e.wait_ge(stt.sem, snapm[2])
                e.wait_ge(st.sem, snapm[3])
            e.wait_ge(sv.sem, prev)
            ld.inc(e.dma_start(out=xldf3[:, 0:TC], in_=io["xn1h"][ct * 128:(ct + 1) * 128, 1:TC + 1]))
        ph.on("sync", ldt)
        ldv = ld.n

        def cst(v, ct=ct, ldv=ldv):
            v.wait_ge(ld.sem, ldv)
            sv.inc(v.tensor_copy(xn1b[:, ct * TC:(ct + 1) * TC], xldf3[:, 0:TC]))
        ph.on("vector", cst)
        cast_done = sv.n
    sv_xn1b = cast_done
    ph.free("xldf3")

    wg = ph.sbuf("wg", [128, NCT * 6], BF16)
    sg_wg = load_weight_bf("wg_ld", io["Wg"], wg, 1024, mdim=6)
    glt = ph.sbuf("glt", [6, 512], F32)
    gtm = ph.sbuf("gtm", [128, 6], F32)
    csel = ph.sbuf("csel", [128, 6], F32)
    csel2 = ph.sbuf("csel2", [128, 6], F32)
    grw = ph.sbuf("grw", [128, 8], F32)
    ctm = ph.sbuf("ctm", [128, 6], BF16)
    combT = ph.sbuf("combT", [6, 128], BF16)
    pg = ph.psum("pg", [6, 512], F32)
    pgt = ph.psum("pgt", [128, 6], F32)
    pgb = ph.psum("pgb", [6, 128], BF16)
    glt_v = [0]
    gtm_users = [0]
    comb_done = [0]
    comb_st = [0]

    for ci, (c0, n) in enumerate(CH_T):
        def mmg(t, c0=c0, n=n, ci=ci):
            t.wait_ge(sg.sem, sg_wg)
            t.wait_ge(sv.sem, sv_xn1b)
            t.wait_ge(sv.sem, glt_v[0])
            i = None
            for ct in range(NCT):
                i = t.matmul(pg[:, :n], wg[:, ct * 6:(ct + 1) * 6], xn1b[:, ct * TC + c0:ct * TC + c0 + n],
                             start=(ct == 0), stop=(ct == NCT - 1))
            stt.inc(i)
        ph.on("tensor", mmg)
        t_g = stt.n

        def evg(v, n=n, t_g=t_g):
            v.wait_ge(stt.sem, t_g)
            v.wait_ge(ld.sem, ld_c)
            sv.inc(v.tensor_scalar(glt[:, :n], pg[0:6, :n], bgc[0:6, :], None, OP.add))
        ph.on("vector", evg)
        sv_g = sv.n
        glt_v[0] = sv_g

        for sb in range(4):
            def tpg_(t, sb=sb, sv_g=sv_g):
                t.wait_ge(sv.sem, max(sv_g, gtm_users[0]))
                stt.inc(t.transpose(pgt[:, :], glt[0:6, sb * 128:(sb + 1) * 128], idf3[0:6, 0:6]))
            ph.on("tensor", tpg_)
            t_tp = stt.n

            def topk(v, sb=sb, c0=c0, t_tp=t_tp):
                v.wait_ge(stt.sem, t_tp)
                D(v, v.tensor_copy(gtm[:, :], pgt[:, 0:6]))
                D(v, v.tensor_reduce(grw[:, 0:1], gtm[:, :], AX.X, OP.max))
                D(v, v.tensor_scalar(grw[:, 4:5], gtm[:, 0:1], 0.0, None, OP.mult))
                D(v, v.tensor_scalar(csel[:, :], gtm[:, :], grw[:, 0:1], None, OP.is_ge))
                D(v, v.scalar_tensor_tensor(csel2[:, :], csel[:, :], -1e30, gtm[:, :], OP.mult, OP.add))
                D(v, v.tensor_reduce(grw[:, 1:2], csel2[:, :], AX.X, OP.max))
                sv.inc(v.tensor_tensor(grw[:, 2:3], grw[:, 0:1], grw[:, 1:2], OP.subtract))
            ph.on("vector", topk)
            sv_m = sv.n

            def sig(a, sv_m=sv_m):
                a.wait_ge(sv.sem, sv_m)
                sa.inc(a.activation(grw[:, 3:4], grw[:, 2:3], AF.Sigmoid))
            ph.on("scalar", sig)
            sa_w1 = sa.n

            def comb(v, sb=sb, sa_w1=sa_w1):
                v.wait_ge(sa.sem, sa_w1)
                D(v, v.tensor_scalar(grw[:, 4:5], grw[:, 3:4], -1.0, 1.0, OP.mult, OP.add))
                D(v, v.tensor_scalar(csel2[:, :], csel2[:, :], grw[:, 1:2], None, OP.is_ge))
                D(v, v.tensor_scalar(csel[:, :], csel[:, :], grw[:, 3:4], None, OP.mult))
                D(v, v.scalar_tensor_tensor(csel[:, :], csel2[:, :], grw[:, 4:5], csel[:, :], OP.mult, OP.add))
                sv.inc(v.tensor_copy(ctm[:, :], csel[:, :]))
            ph.on("vector", comb)
            sv_cb = sv.n
            gtm_users[0] = sv_cb

            def tpb_(t, sb=sb, sv_cb=sv_cb):
                t.wait_ge(sv.sem, sv_cb)
                t.wait_ge(sv.sem, comb_done[0])
                stt.inc(t.transpose(pgb[:, :], ctm[:, :], idb3[:, :]))
            ph.on("tensor", tpb_)
            t_tb = stt.n

            def evcb(v, sb=sb, c0=c0, t_tb=t_tb, st_prev=comb_st[0]):
                v.wait_ge(stt.sem, t_tb)
                v.wait_ge(st.sem, st_prev)
                sv.inc(v.tensor_copy(combT[0:6, :], pgb[0:6, :]))
            ph.on("vector", evcb)
            comb_done[0] = sv.n
            sv_cbw = sv.n

            def stcb(g, sb=sb, c0=c0, sv_cbw=sv_cbw):
                g.wait_ge(sv.sem, sv_cbw)
                st.inc(g.dma_start(out=io["combd"][:, c0 + sb * 128:c0 + (sb + 1) * 128], in_=combT[0:6, :]))
            ph.on("gpsimd", stcb)
            comb_st[0] = st.n
    sv_comb = sv.n

    macc = ph.sbuf("macc", [128, NCT * TC], F32)
    w1e = ph.sbuf("w1e", [128, NCT * 2048], BF16)
    w2e = ph.sbuf("w2e", [128, 16 * 1024], BF16)
    b1e = ph.sbuf("b1e", [128, 16], F32)
    b2e = ph.sbuf("b2e", [128, 8], F32)
    hbb = ph.sbuf("hbb", [128, 16 * 512], BF16)
    bcb = ph.sbuf("bcb", [128, 512], F32)
    tmpy = ph.sbuf("tmpy", [128, 512], F32)
    pbcm = ph.psum("pbcm", [128, 512], F32)
    crow = ph.sbuf("crow", [1, 512], BF16)
    h_a = [0]
    hb_t = [0]
    b_ld = [0]
    macc_v = [0]
    pb_users = [0]

    for e_i in range(6):
        sg_w1 = load_weight_bf(f"w1_{e_i}", io["W1"][e_i], w1e, 1024, mdim=2048)
        sg_w2 = load_weight_bf(f"w2_{e_i}", io["W2"][e_i], w2e, 2048, mdim=1024)

        def ldb(e, e_i=e_i, prev_a=sa.n, prev_v=sv.n):
            e.wait_ge(sa.sem, prev_a)
            e.wait_ge(sv.sem, prev_v)
            ld.inc(e.dma_start(out=b1e[:], in_=io["b1r"][e_i]))
            ld.inc(e.dma_start(out=b2e[:, :], in_=io["b2r"][e_i]))
        ph.on("sync", ldb)
        ld_b = ld.n

        for ci, (c0, n) in enumerate(CH_T):
            def mkcrow(e, e_i=e_i, c0=c0, n=n, prev=pb_users[0], st_cb=comb_st[0]):
                e.wait_ge(st.sem, st_cb)
                e.wait_ge(sv.sem, prev)
                ld.inc(e.dma_start(out=crow[0:1, :n], in_=io["combd"][e_i:e_i + 1, c0:c0 + n]))
            ph.on("sync", mkcrow)
            ld_cr = ld.n

            def bccm(t, n=n, ld_cr=ld_cr, prev=pb_users[0]):
                t.wait_ge(ld.sem, ld_cr)
                t.wait_ge(sv.sem, prev)
                stt.inc(t.matmul(pbcm[:, :n], onesrow[0:1, :], crow[0:1, :n], start=True, stop=True))
            ph.on("tensor", bccm)
            t_bc = stt.n

            def evbc(v, n=n, t_bc=t_bc):
                v.wait_ge(stt.sem, t_bc)
                sv.inc(v.tensor_copy(bcb[:, :n], pbcm[:, :n]))
            ph.on("vector", evbc)
            sv_bc = sv.n
            pb_users[0] = sv_bc

            for ht in range(16):
                bank = pmm[ht % 2]

                def mmh(t, ht=ht, c0=c0, n=n, bank=bank, sg_w1=sg_w1, sg_w2=sg_w2, ld_b=ld_b):
                    t.wait_ge(sg.sem, max(sg_w1, sg_w2))
                    t.wait_ge(ld.sem, ld_b)
                    t.wait_ge(sv.sem, max(sv_xn1b, bank_v.get(id(bank), 0)))
                    t.wait_ge(sa.sem, max(bank_a.get(id(bank), 0), h_a[0] if ht < 2 else 0))
                    i = None
                    for ct in range(NCT):
                        i = t.matmul(bank[:, :n], w1e[:, ct * 2048 + ht * 128:ct * 2048 + (ht + 1) * 128],
                                     xn1b[:, ct * TC + c0:ct * TC + c0 + n],
                                     start=(ct == 0), stop=(ct == NCT - 1))
                    stt.inc(i)
                ph.on("tensor", mmh)
                t_h = stt.n

                def eah(a, ht=ht, n=n, bank=bank, t_h=t_h, prev_y=hb_t[0]):
                    a.wait_ge(stt.sem, max(t_h, prev_y))
                    sa.inc(a.activation(hbb[:, ht * 512:ht * 512 + n], bank[:, :n], AF.Gelu,
                                        bias=b1e[:, ht:ht + 1]))
                ph.on("scalar", eah)
                bank_a[id(bank)] = sa.n
            sa_h = sa.n
            h_a[0] = sa_h

            for ct_out in range(NCT):
                bank = pmm[ct_out % 2]

                def mmy(t, ct_out=ct_out, n=n, bank=bank, sa_h=sa_h):
                    t.wait_ge(sa.sem, sa_h)
                    t.wait_ge(sv.sem, bank_v.get(id(bank), 0))
                    i = None
                    for ht in range(16):
                        i = t.matmul(bank[:, :n], w2e[:, ht * 1024 + ct_out * 128:ht * 1024 + (ct_out + 1) * 128],
                                     hbb[:, ht * 512:ht * 512 + n],
                                     start=(ht == 0), stop=(ht == 15))
                    stt.inc(i)
                ph.on("tensor", mmy)
                t_y = stt.n
                hb_t[0] = stt.n

                def evy(v, e_i=e_i, ct_out=ct_out, c0=c0, n=n, bank=bank, t_y=t_y, sv_bc=sv_bc):
                    v.wait_ge(stt.sem, t_y)
                    v.wait_ge(sv.sem, sv_bc)
                    dst = macc[:, ct_out * TC + c0:ct_out * TC + c0 + n]
                    if e_i == 0:
                        D(v, v.tensor_scalar(tmpy[:, :n], bank[:, :n], b2e[:, ct_out:ct_out + 1], None, OP.add))
                        sv.inc(v.tensor_tensor(dst, tmpy[:, :n], bcb[:, :n], OP.mult))
                    else:
                        D(v, v.tensor_scalar(tmpy[:, :n], bank[:, :n], b2e[:, ct_out:ct_out + 1], None, OP.add))
                        D(v, v.tensor_tensor(tmpy[:, :n], tmpy[:, :n], bcb[:, :n], OP.mult))
                        sv.inc(v.tensor_tensor(dst, dst, tmpy[:, :n], OP.add))
                ph.on("vector", evy)
                bank_v[id(bank)] = sv.n
    sv_moe = sv.n

    f_free = {0: 0, 1: 0}
    ob = [ph.sbuf(f"ob3{i}", [128, 512], F32) for i in range(2)]
    for ct in range(NCT):
        for (c0, n) in CH_T:
            def ldo(e, ct=ct, c0=c0, n=n, prev=f_free[(ct * 4 + c0 // 512) % 2]):
                e.wait_ge(st.sem, max(prev, st_ffn))
                e.wait_ge(sv.sem, sv_moe)
                ld.inc(e.dma_start(out=ob[(ct * 4 + c0 // 512) % 2][:, :n], in_=io["out"][ct * 128:(ct + 1) * 128, c0:c0 + n]))
            ph.on("sync", ldo)
            ld_o = ld.n

            def addm(v, ct=ct, c0=c0, n=n, ld_o=ld_o):
                v.wait_ge(ld.sem, ld_o)
                v.wait_ge(sv.sem, sv_moe)
                par = (ct * 4 + c0 // 512) % 2
                sv.inc(v.tensor_tensor(ob[par][:, :n], ob[par][:, :n], macc[:, ct * TC + c0:ct * TC + c0 + n], OP.add))
            ph.on("vector", addm)
            sv_o = sv.n

            def sto(g, ct=ct, c0=c0, n=n, sv_o=sv_o):
                g.wait_ge(sv.sem, sv_o)
                st.inc(g.dma_start(out=io["out"][ct * 128:(ct + 1) * 128, c0:c0 + n],
                                   in_=ob[(ct * 4 + c0 // 512) % 2][:, :n]))
            ph.on("gpsimd", sto)
            f_free[(ct * 4 + c0 // 512) % 2] = st.n

    def fin(g):
        g.wait_ge(st.sem, st.n)
    ph.on("gpsimd", fin)


def build_phase3():
    nc = bass.Bass("TRN2", target_bir_lowering=False)
    io = {}
    io["rwkvh"] = nc.dram_tensor("rwkvh", [C, TC + 1], BF16, kind="ExternalInput").ap()
    io["xn1h"] = nc.dram_tensor("xn1h", [C, TC + 1], F32, kind="ExternalInput").ap()
    io["x1"] = nc.dram_tensor("x1", [C, TC], F32, kind="ExternalInput").ap()
    io["WoT1"] = nc.dram_tensor("WoT1", [C, C], F32, kind="ExternalInput").ap()
    io["WrfT"] = nc.dram_tensor("WrfT", [C, C], F32, kind="ExternalInput").ap()
    io["WkfT"] = nc.dram_tensor("WkfT", [C, 4096], F32, kind="ExternalInput").ap()
    io["WvfT"] = nc.dram_tensor("WvfT", [4096, C], F32, kind="ExternalInput").ap()
    io["Wg"] = nc.dram_tensor("Wg", [C, 6], F32, kind="ExternalInput").ap()
    io["bg"] = nc.dram_tensor("bg", [6, 1], F32, kind="ExternalInput").ap()
    io["W1"] = nc.dram_tensor("W1", [6, C, 2048], F32, kind="ExternalInput").ap()
    io["W2"] = nc.dram_tensor("W2", [6, 2048, C], F32, kind="ExternalInput").ap()
    io["b1r"] = nc.dram_tensor("b1r", [6, 128, 16], F32, kind="ExternalInput").ap()
    io["b2r"] = nc.dram_tensor("b2r", [6, 128, 8], F32, kind="ExternalInput").ap()
    for nm in ["ln2_g", "ln2_b", "fmk", "fmr", "hm"]:
        io[nm] = nc.dram_tensor(nm, [C, 1], F32, kind="ExternalInput").ap()
    io["idb"] = nc.dram_tensor("idb", [128, 128], BF16, kind="ExternalInput").ap()
    io["idf"] = nc.dram_tensor("idf", [128, 128], F32, kind="ExternalInput").ap()
    io["out"] = nc.dram_tensor("out", [C, TC], F32, kind="ExternalOutput").ap()
    io["xmid"] = nc.dram_tensor("xmid", [C, TC + 1], F32).ap()
    io["kvd"] = nc.dram_tensor("kvd", [C, TC], F32).ap()
    io["combd"] = nc.dram_tensor("combd", [6, TC], BF16).ap()
    with ExitStack() as stack:
        ph = Ph(nc, stack)
        emit_phase3(ph, io)
    return nc


def _prep_phase3(inputs, x1s, xn1s, rkvs):
    i1 = inputs
    b1 = np.asarray(i1["L1_ffn_b1"], np.float32).reshape(6, 16, 128).transpose(0, 2, 1)
    b2 = np.asarray(i1["L1_ffn_b2"], np.float32).reshape(6, 8, 128).transpose(0, 2, 1)
    common = dict(
        WoT1=_hostT(i1["L1_attn_Wo"]), WrfT=_hostT(i1["L1_attn_Wr_ffn"]),
        WkfT=_hostT(i1["L1_attn_Wk_ffn"]), WvfT=_hostT(i1["L1_attn_Wv_ffn"]),
        Wg=np.ascontiguousarray(np.asarray(i1["L1_ffn_Wg"], np.float32)),
        bg=_vec_col(i1["L1_ffn_bg"]),
        W1=np.ascontiguousarray(np.asarray(i1["L1_ffn_W1"], np.float32)),
        W2=np.ascontiguousarray(np.asarray(i1["L1_ffn_W2"], np.float32)),
        b1r=np.ascontiguousarray(b1), b2r=np.ascontiguousarray(b2),
        ln2_g=_vec_col(i1["L1_attn_ln2_g"]), ln2_b=_vec_col(i1["L1_attn_ln2_b"]),
        fmk=_vec_col(i1["L1_attn_ffn_mix_k"]), fmr=_vec_col(i1["L1_attn_ffn_mix_r"]),
        idb=np.eye(128, dtype=np.float32).astype(BF),
        idf=np.eye(128, dtype=np.float32),
    )
    per_core = []
    for i in range(8):
        b, j, t0 = _core_meta(i)
        rkh = np.zeros((C, TC + 1), np.float32)
        rkh[:, 1:] = np.asarray(rkvs[i], np.float32)
        if j > 0:
            rkh[:, 0] = np.asarray(rkvs[i - 1], np.float32)[:, -1]
        xh = np.zeros((C, TC + 1), np.float32)
        xh[:, 1:] = xn1s[i]
        if j > 0:
            xh[:, 0] = xn1s[i - 1][:, -1]
        hm = np.zeros((C, 1), np.float32) if j == 0 else np.ones((C, 1), np.float32)
        per_core.append(dict(
            rwkvh=rkh.astype(BF), xn1h=np.ascontiguousarray(xh),
            x1=np.ascontiguousarray(x1s[i]), hm=hm,
        ))
    return common, per_core


def run_phase3(inputs, x1s, xn1s, rkvs):
    common, per_core = _prep_phase3(inputs, x1s, xn1s, rkvs)
    nc = build_phase3()
    return run_bass_kernel_spmd(nc, [dict(common, **pc) for pc in per_core], core_ids=list(range(8))).results
